# version 15
# baseline (speedup 1.0000x reference)
import numpy as np
import ml_dtypes
import concourse.bass as bass
import concourse.mybir as mybir
from concourse.bass_utils import run_bass_kernel_spmd

F32 = mybir.dt.float32
BF16 = mybir.dt.bfloat16
U8 = mybir.dt.uint8
U32 = mybir.dt.uint32
A = mybir.AluOpType
AF = mybir.ActivationFunctionType
AX = mybir.AxisListType
NP_BF16 = ml_dtypes.bfloat16

T = 2048
H = 2048
NH, NKV, HD = 16, 4, 128
MLP = 16384
NT = T // 128
NHC = H // 128
EPS = 1e-5
SM_SCALE = float(1.0 / np.sqrt(HD))

ENGINES = ("sync", "vector", "scalar", "tensor", "gpsimd")
NODRAIN = False
NOACW = False
NTT = None


class Emitter:
    def __init__(self):
        self.ops = {k: [] for k in ENGINES}
        self.cnt = {}
        for s in ("LDC", "LD2", "LDR", "LDX0", "LDX1", "LDW0", "LDW1", "PE", "DV", "AC", "PO", "ST", "STX0", "STX1", "LDB0", "LDB1", "LDS", "STI", "LDI0", "LDI1", "STO0", "STO1", "STF0", "STF1", "LDJ0", "LDJ1", "STB0", "STB1"):
            self.cnt[s] = 0

    def on(self, eng, fn=None, wait=(), inc=None, tag=""):
        ret = None
        if inc is not None:
            assert fn is not None, f"inc without instruction at {tag}"
            self.cnt[inc[0]] += inc[1]
            ret = self.cnt[inc[0]]
        self.ops[eng].append((fn, tuple(wait), inc, tag))
        return ret

    def wait(self, eng, wait):
        self.ops[eng].append((None, tuple(wait), None, "wait"))

    def drain(self, eng):
        self.ops[eng].append(("DRAIN", (), None, "drain"))

    def dma(self, queue, out_ap, in_ap, sem="LD", wait=(), **kw):
        fn = lambda e, o=out_ap, i=in_ap, kw=kw: e.dma_start(out=o, in_=i, **kw)
        return self.on(queue, fn, wait=wait, inc=(sem, 16))


def build(debug=(), upto="ALL", nchunks=None, nosq=False):
    nc = bass.Bass("TRN2", target_bir_lowering=False)
    em = Emitter()

    x_d = nc.dram_tensor("x", [T, H], F32, kind="ExternalInput").ap()
    wqT_d = nc.dram_tensor("wqT", [H, NH * HD], F32, kind="ExternalInput").ap()
    wkT_d = nc.dram_tensor("wkT", [H, NKV * HD], F32, kind="ExternalInput").ap()
    wvT_d = nc.dram_tensor("wvT", [H, NKV * HD], F32, kind="ExternalInput").ap()
    woT_d = nc.dram_tensor("woT", [NH * HD, H], F32, kind="ExternalInput").ap()
    qw_d = nc.dram_tensor("qw", [1, NH * HD], F32, kind="ExternalInput").ap()
    kw_d = nc.dram_tensor("kw", [1, NKV * HD], F32, kind="ExternalInput").ap()
    gby_d = nc.dram_tensor("gby", [H // 2, MLP], U8, kind="ExternalInput").ap()
    uby_d = nc.dram_tensor("uby", [H // 2, MLP], U8, kind="ExternalInput").ap()
    dby_d = nc.dram_tensor("dby", [MLP // 2, H], U8, kind="ExternalInput").ap()
    gsc_d = nc.dram_tensor("gsc", [H // 2, MLP], F32, kind="ExternalInput").ap()
    usc_d = nc.dram_tensor("usc", [H // 2, MLP], F32, kind="ExternalInput").ap()
    dsc_d = nc.dram_tensor("dsc", [MLP // 2, H], F32, kind="ExternalInput").ap()
    ident_d = nc.dram_tensor("identD", [128, 128], BF16, kind="ExternalInput").ap()
    cosT_d = nc.dram_tensor("cosTD", [128, NT * 64], F32, kind="ExternalInput").ap()
    sinT_d = nc.dram_tensor("sinTD", [128, NT * 64], F32, kind="ExternalInput").ap()
    mask_d = nc.dram_tensor("maskdD", [128, 128], F32, kind="ExternalInput").ap()
    epsc_d = nc.dram_tensor("epscD", [128, 1], F32, kind="ExternalInput").ap()
    consN_d = nc.dram_tensor("consND", [128, 12], F32, kind="ExternalInput").ap()

    x2_h = nc.dram_tensor("x2h", [T, H], F32, kind="Internal").ap()
    intT_h = nc.dram_tensor("intTh", [MLP, T], BF16, kind="Internal").ap()
    out_d = nc.dram_tensor("out", [T, H], F32, kind="ExternalOutput").ap()

    dbg = {}
    def dbg_out(name, shape, dtype):
        ap = nc.dram_tensor("dbg_" + name, shape, dtype, kind="ExternalOutput").ap()
        dbg[name] = ap
        return ap

    ctx = []
    def sbuf(name, shape, dtype):
        g = nc.sbuf_tensor(name, shape, dtype)
        t = g.__enter__(); ctx.append(g); return t
    def psumt(name, shape, dtype=F32):
        g = nc.psum_tensor(name, shape, dtype)
        t = g.__enter__(); ctx.append(g); return t

    ident = sbuf("ident", [128, 128], BF16)
    maskd = sbuf("maskd", [128, 128], F32)
    epsc = sbuf("epsc", [128, 1], F32)
    consN = sbuf("consN", [128, 12], F32)
    qw_e = sbuf("qw_e", [128, NH * HD], BF16)
    kw_e = sbuf("kw_e", [128, NKV * HD], BF16)
    stat = sbuf("stat", [128, 16], F32)
    qk_sums = sbuf("qk_sums", [128, NT * 12], F32)

    R1 = sbuf("R1", [128, 32768], BF16)
    R2 = sbuf("R2", [128, 32768], BF16)
    R3 = sbuf("R3", [128, 8192], BF16)
    R4 = sbuf("R4", [128, 8192], BF16)
    scr = sbuf("scr", [128, 40960], U8)

    def scrv(off_bytes, n, dtype):
        b = scr[:].bitcast(dtype)
        esz = {F32: 4, BF16: 2, U8: 1, U32: 4}[dtype]
        assert off_bytes % esz == 0 and off_bytes + n * esz <= 40960
        return b[:, off_bytes // esz: off_bytes // esz + n]

    xlnT = R1[:]
    q_sb = R2[:]
    v_sb = R4[:]

    xs = [scrv(0, 2048, F32), scrv(8192, 2048, F32)]
    kT = scrv(0, 8192, BF16)
    k_st = R3[:]
    qn_s = scrv(16384, 2048, BF16)
    rp_a = scrv(20480, 1024, BF16)
    rp_b = scrv(22528, 1024, BF16)
    xln_s = scrv(24576, 2048, BF16)
    wst = [scrv(16384, 4096, BF16), scrv(28672, 4096, BF16)]
    cosr = scrv(28672, 1024, F32)
    sinr = scrv(32768, 1024, F32)

    PS = psumt("PS", [128, 4096], F32)
    def ps_bank(i, n=1):
        return PS[:, i * 512:(i + n) * 512]
    ps_tp = [ps_bank(0).bitcast(BF16)[:, 0:128], ps_bank(1).bitcast(BF16)[:, 0:128]]
    ps_q = ps_bank(2)[:, 0:256]
    ps_k = ps_bank(3)[:, 0:256]
    ps_v = ps_bank(4)[:, 0:256]

    tp_guard = [0, 0]
    xs_guard = [0, 0]
    ps_guard = {"q": [], "k": [], "v": []}

    em.dma("sync", ident[:], ident_d, sem="LDC")
    em.dma("sync", maskd[:], mask_d, sem="LDC")
    em.dma("sync", epsc[:], epsc_d, sem="LDC")
    em.dma("sync", consN[:], consN_d, sem="LDC")
    em.dma("gpsimd", qw_e[:], bass.AP(qw_d.tensor, 0, [[0, 128], [1, NH * HD]]), sem="LD2")
    em.dma("gpsimd", kw_e[:], bass.AP(kw_d.tensor, 0, [[0, 128], [1, NKV * HD]]), sem="LD2")
    c_ld = em.cnt["LDC"]
    c_ld2 = em.cnt["LD2"]

    xr_guard = [0, 0]
    sq_prev = 0
    for tt in range(NT):
        ttp = tt % 2
        sb0 = 0 if ttp == 0 else 8
        xr = R3[:, ttp * 2048:(ttp + 1) * 2048]
        dump = R3[:, 4096 + ttp * 2048: 6144 + ttp * 2048]
        xb = xs[ttp]
        lsem = "LDX0" if ttp == 0 else "LDX1"
        t_x = em.dma("sync", xb, x_d[tt * 128:(tt + 1) * 128, :], sem=lsem,
                     wait=[("DV", xs_guard[ttp])] if xs_guard[ttp] else ())
        t_red = em.on("vector", lambda v, xb=xb, sb0=sb0: v.tensor_reduce(stat[:, sb0:sb0 + 1], xb, AX.X, A.add),
              wait=[(lsem, t_x)], inc=("DV", 1), tag=f"ln1sum{tt}")
        t_sq = em.on("scalar", lambda s, xb=xb, dump=dump, sb0=sb0: s.activation(
            out=dump, in_=xb, func=AF.Square, accum_out=stat[:, sb0 + 1:sb0 + 2]),
                     wait=[(lsem, t_x)] + ([("AC", sq_prev)] if sq_prev else []),
                     inc=("AC", 1), tag=f"ln1sq{tt}")
        sq_prev = t_sq
        t_mean = em.on("vector", lambda v, sb0=sb0: v.tensor_scalar(
            stat[:, sb0 + 2:sb0 + 3], stat[:, sb0:sb0 + 1], 1.0 / H, None, A.mult),
            wait=[("DV", t_red)], inc=("DV", 1))
        t_mm2 = em.on("vector", lambda v, sb0=sb0: v.tensor_tensor(
            stat[:, sb0 + 3:sb0 + 4], stat[:, sb0 + 2:sb0 + 3], stat[:, sb0 + 2:sb0 + 3], A.mult),
            wait=[("DV", t_mean)], inc=("DV", 1))
        t_var = em.on("vector", lambda v, sb0=sb0: v.scalar_tensor_tensor(
            stat[:, sb0 + 4:sb0 + 5], stat[:, sb0 + 1:sb0 + 2], 1.0 / H,
            stat[:, sb0 + 3:sb0 + 4], A.mult, A.subtract),
                      wait=[("AC", t_sq), ("DV", t_mm2)], inc=("DV", 1))
        t_sd = em.on("scalar", lambda s, sb0=sb0: s.activation(
            out=stat[:, sb0 + 5:sb0 + 6], in_=stat[:, sb0 + 4:sb0 + 5], func=AF.Sqrt, bias=epsc[:]),
                     wait=[("DV", t_var), ("LDC", c_ld)], inc=("AC", 1))
        t_rc = em.on("vector", lambda v, sb0=sb0: v.reciprocal(
            stat[:, sb0 + 6:sb0 + 7], stat[:, sb0 + 5:sb0 + 6]), wait=[("AC", t_sd)], inc=("DV", 1))
        wxl = [("DV", t_rc)]
        if xr_guard[ttp]:
            wxl.append(("PE", xr_guard[ttp]))
        t_xln = em.on("vector", lambda v, xb=xb, xr=xr, sb0=sb0: v.tensor_scalar(
            xr, xb, stat[:, sb0 + 2:sb0 + 3], stat[:, sb0 + 6:sb0 + 7], A.subtract, A.mult),
            wait=wxl, inc=("DV", 1))
        xs_guard[ttp] = t_xln
        for hc in range(NHC):
            pt = ps_tp[hc % 2]
            src = xr[:, hc * 128:(hc + 1) * 128]
            dst = xlnT[:, hc * T + tt * 128: hc * T + (tt + 1) * 128]
            wt = [("DV", t_xln)]
            if tt == 0 and hc < 2:
                wt.append(("LDC", c_ld))
            if tp_guard[hc % 2]:
                wt.append(("DV", tp_guard[hc % 2]))
            t_pe = em.on("tensor", lambda t, pt=pt, src=src: t.transpose(pt, src, ident[:]),
                         wait=wt, inc=("PE", 1), tag=f"tpA1_{tt}_{hc}")
            tp_guard[hc % 2] = em.on("vector", lambda v, pt=pt, dst=dst: v.tensor_copy(dst, pt),
                                     wait=[("PE", t_pe)], inc=("DV", 1), tag=f"cpA1_{tt}_{hc}")
        xr_guard[ttp] = em.cnt["PE"]
    dv_a1 = em.cnt["DV"]

    if "xlnT" in debug:
        em.dma("sync", dbg_out("xlnT", [128, NHC * T], BF16), xlnT, sem="ST", wait=[("DV", dv_a1)])

    if upto == "A1":
        em.wait("sync", [("ST", em.cnt["ST"]), ("DV", em.cnt["DV"]), ("AC", em.cnt["AC"]),
                         ("PE", em.cnt["PE"]), ("LD2", em.cnt["LD2"])])
        _replay(nc, em)
        for g in reversed(ctx):
            g.__exit__(None, None, None)
        return nc, dbg

    CHUNKS = ([("q", oc, wqT_d, NH * HD, oc) for oc in range(8)]
              + [("k", oc, wkT_d, NKV * HD, 8 + oc) for oc in range(2)]
              + [("v", oc, wvT_d, NKV * HD, None) for oc in range(2)])
    if nchunks is not None:
        CHUNKS = CHUNKS[:nchunks]
    wst_guard = [(), ()]
    for ci, (which, oc, w_dram, w_cols, sumslot) in enumerate(CHUNKS):
        wb = wst[ci % 2]
        wsem = "LDW0" if ci % 2 == 0 else "LDW1"
        t_w = em.dma("gpsimd", wb,
                     bass.AP(w_dram.tensor, oc * 256, [[w_cols, 128], [128 * w_cols, NHC], [1, 256]]),
                     sem=wsem, wait=wst_guard[ci % 2])
        ps = {"q": ps_q, "k": ps_k, "v": ps_v}[which]
        for tt in range(NTT if NTT else NT):
            for hc in range(NHC):
                lhs = xlnT[:, hc * T + tt * 128: hc * T + (tt + 1) * 128]
                rhs = wb[:, hc * 256:(hc + 1) * 256]
                wt = []
                if hc == 0:
                    wt.append((wsem, t_w))
                    if ci == 0 and tt == 0:
                        wt.append(("DV", dv_a1))
                    wt.extend(ps_guard[which])
                inc = ("PE", 1) if hc == NHC - 1 else None
                em.on("tensor", lambda t, lhs=lhs, rhs=rhs, hc=hc, ps=ps: t.matmul(
                    ps, lhs, rhs, start=(hc == 0), stop=(hc == NHC - 1)),
                      wait=wt, inc=inc, tag=f"mm_{which}{oc}_{tt}_{hc}")
            t_mm = em.cnt["PE"]
            if which == "q":
                dst = q_sb[:, tt * 2048 + oc * 256: tt * 2048 + oc * 256 + 256]
            elif which == "k":
                dst = k_st[:, tt * 512 + oc * 256: tt * 512 + oc * 256 + 256]
            else:
                dst = v_sb[:, tt * 512 + oc * 256: tt * 512 + oc * 256 + 256]
            t_ev = em.on("vector", lambda v, dst=dst, ps=ps: v.tensor_copy(dst, ps),
                         wait=[("PE", t_mm)], inc=("DV", 1), tag=f"ev_{which}{oc}_{tt}")
            ps_guard[which] = [("DV", t_ev)]
        wst_guard[ci % 2] = [("PE", em.cnt["PE"])]
    ac_proj = em.cnt["AC"]
    pe_qkv = em.cnt["PE"]

    if upto == "A2":
        if "v" in debug:
            em.dma("sync", dbg_out("v", [128, NT * 512], BF16), v_sb, sem="ST", wait=[("DV", em.cnt["DV"])])
        if "q_sb" in debug:
            em.dma("sync", dbg_out("q_sb", [128, NT * 2048], BF16), q_sb, sem="ST", wait=[("DV", em.cnt["DV"])])
        em.wait("sync", [("ST", em.cnt["ST"]), ("DV", em.cnt["DV"]), ("AC", em.cnt["AC"]),
                         ("PE", em.cnt["PE"]), ("LD2", em.cnt["LD2"])])
        _replay(nc, em)
        for g in reversed(ctx):
            g.__exit__(None, None, None)
        return nc, dbg

    t_cos = em.dma("sync", cosr, cosT_d, sem="LDR", wait=[("PE", pe_qkv)])
    t_sin = em.dma("sync", sinr, sinT_d, sem="LDR", wait=[("PE", pe_qkv)])

    qT = xlnT

    def rope_emit(src_ap, dst_ap, tt, nheads, tag):
        d2 = HD // 2
        s3 = src_ap.rearrange("p (h two d) -> p h two d", two=2, d=d2)
        o3 = dst_ap.rearrange("p (h two d) -> p h two d", two=2, d=d2)
        ch = cosr[:, tt * d2: (tt + 1) * d2]
        snh = sinr[:, tt * d2: (tt + 1) * d2]
        cb = ch.unsqueeze(1).unsqueeze(2).broadcast_to([128, nheads, 2, d2])
        sb_ = snh.unsqueeze(1).unsqueeze(2).broadcast_to([128, nheads, 2, d2])
        ra = rp_a[:, 0:nheads * d2].rearrange("p (h d) -> p h d", d=d2)
        rb = rp_b[:, 0:nheads * d2].rearrange("p (h d) -> p h d", d=d2)
        em.on("vector", lambda v: v.tensor_tensor(o3[:, :, 0, :], s3[:, :, 0, :], cb[:, :, 0, :], A.mult), tag=tag + "a")
        em.on("vector", lambda v: v.tensor_tensor(o3[:, :, 1, :], s3[:, :, 1, :], cb[:, :, 1, :], A.mult), tag=tag + "b")
        em.on("vector", lambda v: v.tensor_tensor(ra, s3[:, :, 1, :], sb_[:, :, 0, :], A.mult), tag=tag + "c")
        em.on("vector", lambda v: v.tensor_tensor(rb, s3[:, :, 0, :], sb_[:, :, 1, :], A.mult), tag=tag + "d")
        em.drain("vector")
        em.on("vector", lambda v: v.tensor_tensor(o3[:, :, 0, :], o3[:, :, 0, :], ra, A.subtract), tag=tag + "e")
        return em.on("vector", lambda v: v.tensor_tensor(o3[:, :, 1, :], o3[:, :, 1, :], rb, A.add),
                     inc=("DV", 1), tag=tag + "f")

    for tt in range(NT):
        qrow = q_sb[:, tt * 2048:(tt + 1) * 2048]
        krow0 = k_st[:, tt * 512:(tt + 1) * 512]
        em.on("vector", lambda v, krow0=krow0: v.tensor_tensor(qn_s[:, 0:512], krow0, krow0, A.mult),
              tag=f"sqk_{tt}")
        em.on("vector", lambda v, qrow=qrow: v.tensor_tensor(xln_s, qrow, qrow, A.mult), tag=f"sqq_{tt}")
        em.drain("vector")
        em.on("vector", lambda v: v.tensor_reduce(stat[:, 3:4], qn_s[:, 0:512], AX.X, A.add), tag=f"redk_{tt}")
        em.on("vector", lambda v: v.tensor_reduce(stat[:, 0:1], xln_s, AX.X, A.add), tag=f"redq_{tt}")
        em.drain("vector")
        t_q = em.on("vector", lambda v: v.tensor_scalar(stat[:, 0:1], stat[:, 0:1], 1.0 / (NH * HD), None,
                                                        A.mult), inc=("DV", 1))
        t_k = em.on("vector", lambda v: v.tensor_scalar(stat[:, 3:4], stat[:, 3:4],
                                                        1.0 / (NKV * HD), None, A.mult), inc=("DV", 1))
        t_qs = em.on("scalar", lambda s: s.activation(out=stat[:, 1:2], in_=stat[:, 0:1], func=AF.Sqrt,
                                                      bias=epsc[:]), wait=[("DV", t_k)], inc=("AC", 1))
        t_ks = em.on("scalar", lambda s: s.activation(out=stat[:, 4:5], in_=stat[:, 3:4], func=AF.Sqrt,
                                                      bias=epsc[:]), wait=[("DV", t_k)], inc=("AC", 1))
        em.on("vector", lambda v: v.reciprocal(stat[:, 2:3], stat[:, 1:2]), wait=[("AC", t_ks)])
        em.on("vector", lambda v: v.reciprocal(stat[:, 5:6], stat[:, 4:5]))
        em.drain("vector")
        em.on("vector", lambda v, qrow=qrow: v.tensor_scalar(qn_s, qrow, stat[:, 2:3], None, A.mult),
              tag=f"qn1_{tt}")
        em.drain("vector")
        em.on("vector", lambda v: v.tensor_tensor(qn_s, qn_s, qw_e[:], A.mult),
              wait=[("LD2", c_ld2), ("LDR", t_sin)] if tt == 0 else (), tag=f"qn2_{tt}")
        em.drain("vector")
        t_rope = rope_emit(qn_s, xln_s, tt, NH, f"ropeq_{tt}")
        for hq in range(NH):
            pt = ps_tp[hq % 2]
            src = xln_s[:, hq * 128:(hq + 1) * 128]
            dst = qT[:, hq * T + tt * 128: hq * T + (tt + 1) * 128]
            wt = [("DV", t_rope), ("DV", tp_guard[hq % 2])]
            t_pe = em.on("tensor", lambda t, pt=pt, src=src: t.transpose(pt, src, ident[:]),
                         wait=wt, inc=("PE", 1), tag=f"tpQ_{tt}_{hq}")
            tp_guard[hq % 2] = em.on("vector", lambda v, pt=pt, dst=dst: v.tensor_copy(dst, pt),
                                     wait=[("PE", t_pe)], inc=("DV", 1))
        pe_q_tp = em.cnt["PE"]
        krow = k_st[:, tt * 512:(tt + 1) * 512]
        em.on("vector", lambda v, krow=krow: v.tensor_scalar(qn_s[:, 0:512], krow, stat[:, 5:6], None, A.mult))
        em.drain("vector")
        em.on("vector", lambda v: v.tensor_tensor(qn_s[:, 0:512], qn_s[:, 0:512], kw_e[:], A.mult))
        em.drain("vector")
        em.wait("vector", [("PE", pe_q_tp)])
        t_ropek = rope_emit(qn_s[:, 0:512], xln_s[:, 0:512], tt, NKV, f"ropek_{tt}")
        for g in range(NKV):
            pt = ps_tp[g % 2]
            src = xln_s[:, g * 128:(g + 1) * 128]
            dst = kT[:, g * T + tt * 128: g * T + (tt + 1) * 128]
            t_pe = em.on("tensor", lambda t, pt=pt, src=src: t.transpose(pt, src, ident[:]),
                         wait=[("DV", t_ropek), ("DV", tp_guard[g % 2])], inc=("PE", 1), tag=f"tpK_{tt}_{g}")
            tp_guard[g % 2] = em.on("vector", lambda v, pt=pt, dst=dst: v.tensor_copy(dst, pt),
                                    wait=[("PE", t_pe)], inc=("DV", 1))
        em.wait("vector", [("PE", em.cnt["PE"])])


    if upto == "A2b":
        if "qT" in debug:
            em.dma("sync", dbg_out("qT", [128, NH * T], BF16), qT, sem="ST", wait=[("DV", em.cnt["DV"])])
        if "kT" in debug:
            em.dma("sync", dbg_out("kT", [128, NKV * T], BF16), kT, sem="ST", wait=[("DV", em.cnt["DV"])])
        if "v" in debug:
            em.dma("sync", dbg_out("v", [128, NT * 512], BF16), v_sb, sem="ST", wait=[("DV", em.cnt["DV"])])
        em.wait("sync", [("ST", em.cnt["ST"]), ("DV", em.cnt["DV"]), ("AC", em.cnt["AC"]),
                         ("PE", em.cnt["PE"]), ("LD2", em.cnt["LD2"])])
        _replay(nc, em)
        for g in reversed(ctx):
            g.__exit__(None, None, None)
        return nc, dbg

    dv_a2b = em.cnt["DV"]
    attT = q_sb
    P_rows = [xln_s, scrv(28672, 2048, BF16)]
    pstage = [scrv(16384 + 256 * i, 128, BF16) for i in range(4)]
    ostage = scrv(17408, 128, BF16)
    ps_sc = PS[:, 0:2048]
    ps_pt2 = [ps_bank(4).bitcast(BF16)[:, 0:128], ps_bank(5).bitcast(BF16)[:, 0:128]]
    ps_oa = PS[:, 6 * 512: 6 * 512 + 128]
    ps_ot = ps_bank(7).bitcast(BF16)[:, 0:128]
    smsc_ap = consN[:, 8:9]

    a3 = {"sc": 0, "prow": [0, 0], "ppt": [0, 0], "pst": [0, 0, 0, 0],
          "oa": 0, "ost": 0, "psot": 0, "rs": [0, 0]}

    def emit_vloop(hq, qt, g, nk, t_e, pb):
        P_row = P_rows[pb]
        sum_c = 14 if pb == 0 else 12
        t_rs = em.on("vector", lambda v, sum_c=sum_c: v.reciprocal(stat[:, sum_c + 1:sum_c + 2],
                                                                  stat[:, sum_c:sum_c + 1]),
                     wait=[("AC", t_e)], inc=("DV", 1))
        a3["rs"][pb] = t_rs
        tp_of = {}

        def emit_tp(kt):
            par = kt % 2
            wt_ = [("AC", t_e)]
            if a3["ppt"][par]:
                wt_.append(("DV", a3["ppt"][par]))
            tp_of[kt] = em.on("tensor", lambda t, kt=kt, par=par: t.transpose(
                ps_pt2[par], P_row[:, kt * 128:(kt + 1) * 128], ident[:]),
                  wait=wt_, inc=("PE", 1), tag=f"ptp_{hq}_{qt}_{kt}")

        emit_tp(0)
        if nk > 1:
            emit_tp(1)
        t_mm = 0
        for kt in range(nk):
            par = kt % 2
            wt2 = [("PE", tp_of[kt])]
            if a3["pst"][kt % 4]:
                wt2.append(("PE", a3["pst"][kt % 4]))
            t_cp = em.on("vector", lambda v, par=par, kt=kt: v.tensor_copy(pstage[kt % 4], ps_pt2[par]),
                         wait=wt2, inc=("DV", 1))
            a3["ppt"][par] = t_cp
            wtm = [("DV", t_cp)]
            if kt == 0 and a3["oa"]:
                wtm.append(("DV", a3["oa"]))
            t_mm = em.on("tensor", lambda t, kt=kt, g=g, nk=nk: t.matmul(
                ps_oa, pstage[kt % 4], v_sb[:, kt * 512 + g * 128: kt * 512 + (g + 1) * 128],
                start=(kt == 0), stop=(kt == nk - 1)), wait=wtm, inc=("PE", 1), tag=f"vm_{hq}_{qt}_{kt}")
            a3["pst"][kt % 4] = t_mm
            if kt + 2 < nk:
                emit_tp(kt + 2)
        a3["prow"][pb] = tp_of[nk - 1] if nk >= 1 else 0
        wt3 = [("PE", t_mm), ("DV", t_rs)]
        if a3["ost"]:
            wt3.append(("PE", a3["ost"]))
        rs_c = 15 if pb == 0 else 13
        t_ev = em.on("vector", lambda v, rs_c=rs_c: v.tensor_scalar(ostage, ps_oa, stat[:, rs_c:rs_c + 1],
                                                                    None, A.mult),
                     wait=wt3, inc=("DV", 1), tag=f"oev_{hq}_{qt}")
        a3["oa"] = t_ev
        wt4 = [("DV", t_ev)]
        if a3["psot"]:
            wt4.append(("DV", a3["psot"]))
        t_ot = em.on("tensor", lambda t: t.transpose(ps_ot, ostage, ident[:]),
                     wait=wt4, inc=("PE", 1))
        a3["ost"] = t_ot
        a3["psot"] = em.on("vector", lambda v, hq=hq, qt=qt: v.tensor_copy(
            attT[:, hq * 2048 + qt * 128: hq * 2048 + (qt + 1) * 128], ps_ot),
              wait=[("PE", t_ot)], inc=("DV", 1))

    pending = None
    itc = 0
    for hq in range(NH):
        g = hq // 4
        for qt in range(NT):
            nk = qt + 1
            ncols = nk * 128
            pb = itc % 2
            lhs = qT[:, hq * T + qt * 128: hq * T + (qt + 1) * 128]
            c = 0
            first = True
            while c < ncols:
                w = min(512, ncols - c)
                wt = []
                if first:
                    if a3["sc"]:
                        wt.append(("AC", a3["sc"]))
                    if itc == 0:
                        wt.append(("DV", dv_a2b))
                    first = False
                inc = ("PE", 1) if c + w >= ncols else None
                em.on("tensor", lambda t, c=c, w=w, lhs=lhs, g=g: t.matmul(
                    ps_sc[:, c:c + w], lhs, kT[:, g * T + c: g * T + c + w], start=True, stop=True),
                      wait=wt, inc=inc, tag=f"sc_{hq}_{qt}_{c}")
                c += w
            t_sc = em.cnt["PE"]
            t_max = em.on("vector", lambda v, ncols=ncols: v.reduce_max(stat[:, 8:9], ps_sc[:, 0:ncols], AX.X),
                          wait=[("PE", t_sc)], inc=("DV", 1), tag=f"mxc_{hq}_{qt}")
            t_msk = em.on("vector", lambda v, nk=nk: v.tensor_tensor(
                ps_sc[:, (nk - 1) * 128: nk * 128], ps_sc[:, (nk - 1) * 128: nk * 128], maskd[:], A.add),
                  wait=[("DV", t_max)], inc=("DV", 1), tag=f"msk_{hq}_{qt}")
            t_bias = em.on("vector", lambda v: v.tensor_scalar(stat[:, 9:10], stat[:, 8:9], -SM_SCALE,
                                                               None, A.mult), wait=[("DV", t_max)], inc=("DV", 1))
            wt_exp = [("DV", t_bias), ("DV", t_msk)]
            if a3["prow"][pb]:
                wt_exp.append(("PE", a3["prow"][pb]))
            if a3["rs"][pb]:
                wt_exp.append(("DV", a3["rs"][pb]))
            sum_c = 14 if pb == 0 else 12
            t_e = em.on("scalar", lambda s, ncols=ncols, pb=pb, sum_c=sum_c: s.activation(
                out=P_rows[pb][:, 0:ncols], in_=ps_sc[:, 0:ncols], func=AF.Exp, bias=stat[:, 9:10],
                scale=smsc_ap, accum_out=stat[:, sum_c:sum_c + 1]), wait=wt_exp, inc=("AC", 1),
                  tag=f"exp_{hq}_{qt}")
            a3["sc"] = t_e
            if pending is not None:
                emit_vloop(*pending)
            pending = (hq, qt, g, nk, t_e, pb)
            itc += 1
    emit_vloop(*pending)
    dv_a3 = em.cnt["DV"]
    pe_a3 = em.cnt["PE"]

    if upto == "A3":
        if "attT" in debug:
            em.dma("sync", dbg_out("attT", [128, NH * T], BF16), attT, sem="ST", wait=[("DV", dv_a3)])
        em.wait("sync", [("ST", em.cnt["ST"]), ("DV", em.cnt["DV"]), ("AC", em.cnt["AC"]),
                         ("PE", em.cnt["PE"]), ("LD2", em.cnt["LD2"])])
        _replay(nc, em)
        for g in reversed(ctx):
            g.__exit__(None, None, None)
        return nc, dbg

    woA = k_st
    woB = v_sb
    xt2 = [scrv(16384, 1024, F32), scrv(20480, 1024, F32)]
    x2st = [scrv(24576, 1024, F32), scrv(28672, 1024, F32)]
    sqdump = scrv(32768, 1024, F32)
    ps_o = PS[:, 0:1024]
    ln2p = qk_sums

    wo_guard = [0, 0]
    xt_guard = [0, 0]
    x2_guard = [0, 0]
    pso_guard = 0
    for hh in range(2):
        wA = [("PE", wo_guard[0])] if wo_guard[0] else [("DV", dv_a3)]
        t_woA = em.dma("gpsimd", woA, bass.AP(woT_d.tensor, hh * 1024,
                       [[H, 128], [128 * H, 8], [1, 1024]]), sem="LDW0", wait=wA)
        t_woB = em.dma("gpsimd", woB, bass.AP(woT_d.tensor, 8 * 128 * H + hh * 1024,
                       [[H, 128], [128 * H, 8], [1, 1024]]), sem="LDW1", wait=wA)
        for tt in range(NT):
            par = tt % 2
            lsem = "LDX0" if par == 0 else "LDX1"
            t_x = em.dma("sync", xt2[par], x_d[tt * 128:(tt + 1) * 128, hh * 1024:(hh + 1) * 1024],
                         sem=lsem, wait=[("DV", xt_guard[par])] if xt_guard[par]
                         else [("DV", dv_a3), ("PE", pe_a3)])
            for oc in range(16):
                wot = (woA if oc < 8 else woB)[:, (oc % 8) * 1024:((oc % 8) + 1) * 1024]
                for h2 in range(2):
                    wt = []
                    if oc == 0 and h2 == 0:
                        wt = [("LDW0", t_woA), ("LDW1", t_woB)]
                        if pso_guard:
                            wt.append(("DV", pso_guard))
                        if hh == 0 and tt == 0:
                            wt.append(("DV", dv_a3))
                    inc = ("PE", 1) if (oc == 15 and h2 == 1) else None
                    em.on("tensor", lambda t, oc=oc, h2=h2, wot=wot, tt=tt: t.matmul(
                        ps_o[:, h2 * 512:(h2 + 1) * 512],
                        attT[:, oc * 2048 + tt * 128: oc * 2048 + (tt + 1) * 128],
                        wot[:, h2 * 512:(h2 + 1) * 512], start=(oc == 0), stop=(oc == 15)),
                          wait=wt, inc=inc, tag=f"wo_{hh}_{tt}_{oc}_{h2}")
            t_pe = em.cnt["PE"]
            wt5 = [("PE", t_pe), (lsem, t_x)]
            if x2_guard[par]:
                wt5.append(("STX0" if par == 0 else "STX1", x2_guard[par]))
            t_add = em.on("vector", lambda v, par=par: v.tensor_tensor(x2st[par], ps_o, xt2[par], A.add),
                          wait=wt5, inc=("DV", 1), tag=f"x2_{hh}_{tt}")
            pso_guard = t_add
            xt_guard[par] = t_add
            em.drain("vector")
            em.on("vector", lambda v, par=par, tt=tt, hh=hh: v.tensor_reduce(
                ln2p[:, tt * 4 + hh: tt * 4 + hh + 1], x2st[par], AX.X, A.add))
            em.on("vector", lambda v, par=par: v.tensor_tensor(sqdump, x2st[par], x2st[par], A.mult))
            em.drain("vector")
            t_st2 = em.on("vector", lambda v, tt=tt, hh=hh: v.tensor_reduce(
                ln2p[:, tt * 4 + 2 + hh: tt * 4 + 3 + hh], sqdump, AX.X, A.add), inc=("DV", 1))
            x2_guard[par] = em.dma("sync", x2_h[tt * 128:(tt + 1) * 128, hh * 1024:(hh + 1) * 1024],
                                   x2st[par], sem="STX0" if par == 0 else "STX1", wait=[("DV", t_st2)])
        wo_guard = [em.cnt["PE"], em.cnt["PE"]]
    st_x2 = (em.cnt["STX0"], em.cnt["STX1"])
    dv_a4 = em.cnt["DV"]

    xfT = R1[:]
    x2full = scrv(16384, 2048, F32)
    xf_s = scrv(24576, 2048, BF16)
    x5_guard = 0
    for tt in range(NT):
        wt = [("STX0", st_x2[0]), ("STX1", st_x2[1])] if tt == 0 else []
        if x5_guard:
            wt.append(("DV", x5_guard))
        t_x2 = em.dma("sync", x2full, x2_h[tt * 128:(tt + 1) * 128, :], sem="LDX0", wait=wt)
        c0 = tt * 4
        em.on("vector", lambda v, c0=c0: v.tensor_tensor(stat[:, 0:1], ln2p[:, c0:c0 + 1],
                                                         ln2p[:, c0 + 1:c0 + 2], A.add),
              wait=[("DV", dv_a4)] if tt == 0 else ())
        em.on("vector", lambda v, c0=c0: v.tensor_tensor(stat[:, 1:2], ln2p[:, c0 + 2:c0 + 3],
                                                         ln2p[:, c0 + 3:c0 + 4], A.add))
        em.drain("vector")
        em.on("vector", lambda v: v.tensor_scalar(stat[:, 2:3], stat[:, 0:1], 1.0 / H, None, A.mult))
        em.drain("vector")
        em.on("vector", lambda v: v.tensor_tensor(stat[:, 3:4], stat[:, 2:3], stat[:, 2:3], A.mult))
        em.drain("vector")
        t_var = em.on("vector", lambda v: v.scalar_tensor_tensor(stat[:, 4:5], stat[:, 1:2], 1.0 / H,
                                                                 stat[:, 3:4], A.mult, A.subtract),
                      inc=("DV", 1))
        t_sd = em.on("scalar", lambda s: s.activation(out=stat[:, 5:6], in_=stat[:, 4:5], func=AF.Sqrt,
                                                      bias=epsc[:]), wait=[("DV", t_var)], inc=("AC", 1))
        em.on("vector", lambda v: v.reciprocal(stat[:, 6:7], stat[:, 5:6]), wait=[("AC", t_sd)])
        em.drain("vector")
        t_xf = em.on("vector", lambda v: v.tensor_scalar(xf_s, x2full, stat[:, 2:3], stat[:, 6:7],
                                                         A.subtract, A.mult),
                     wait=[("LDX0", t_x2)], inc=("DV", 1))
        x5_guard = t_xf
        for c8 in range(8):
            for parn in range(2):
                kk = 2 * c8 + parn
                pt = ps_tp[kk % 2]
                src = xf_s[:, 256 * c8 + parn: 256 * c8 + 256: 2]
                dst = xfT[:, kk * T + tt * 128: kk * T + (tt + 1) * 128]
                wt6 = [("DV", t_xf)]
                if tp_guard[kk % 2]:
                    wt6.append(("DV", tp_guard[kk % 2]))
                t_pe = em.on("tensor", lambda t, pt=pt, src=src: t.transpose(pt, src, ident[:]),
                             wait=wt6, inc=("PE", 1), tag=f"tpXF_{tt}_{kk}")
                tp_guard[kk % 2] = em.on("vector", lambda v, pt=pt, dst=dst: v.tensor_copy(dst, pt),
                                         wait=[("PE", t_pe)], inc=("DV", 1))
        em.wait("vector", [("PE", em.cnt["PE"])])
    dv_a5 = em.cnt["DV"]

    if upto == "A5":
        if "xfT" in debug:
            em.dma("sync", dbg_out("xfT", [128, NHC * T], BF16), xfT, sem="ST", wait=[("DV", dv_a5)])
        if "x2" in debug:
            em.dma("sync", dbg_out("x2", [T, H], F32), x2_h, sem="ST",
                   wait=[("STX0", st_x2[0]), ("STX1", st_x2[1])])
        em.wait("sync", [("ST", em.cnt["ST"]), ("DV", em.cnt["DV"]), ("AC", em.cnt["AC"]),
                         ("PE", em.cnt["PE"]), ("LD2", em.cnt["LD2"])])
        _replay(nc, em)
        for g in reversed(ctx):
            g.__exit__(None, None, None)
        return nc, dbg


    pe_a5 = em.cnt["PE"]
    ext_s = [[scrv(o, 256, U32), scrv(o + 1024, 256, U32), scrv(o + 2048, 256, U32)]
             for o in (0, 3072)]
    ext_u8 = [[scrv(o, 1024, U8), scrv(o + 1024, 1024, U8), scrv(o + 2048, 1024, U8)]
              for o in (0, 3072)]
    tmp = [[scrv(o, 1024, BF16), scrv(o + 2048, 1024, BF16), scrv(o + 4096, 1024, BF16),
            scrv(o + 6144, 1024, BF16)] for o in (6144, 14336)]
    interA = scrv(26624, 2048, BF16)
    interB = scrv(30720, 2048, BF16)
    R3u8 = R3[:].bitcast(U8)
    R2b = R2[:]
    ps_g = PS[:, 0:2048]
    ps_u = PS[:, 2048:4096]

    cHALF = consN[:, 0:1]; cM2 = consN[:, 1:2]; cM6 = consN[:, 2:3]
    c32 = consN[:, 3:4]; c16 = consN[:, 4:5]; cM1 = consN[:, 5:6]
    c4 = consN[:, 6:7]; c127 = consN[:, 7:8]; cM4 = consN[:, 9:10]
    LO_SC = ((None, cM4), (None, cM6))
    HI_SC = ((c16, cM4), (c16, cM6))

    extr_guard = [0, 0]
    scl_guard = 0
    valset_guard = [pe_a5, pe_a5]
    tmpset_guard = [0, 0]
    st_dq = {"scl": 0, "valwait": None}

    def emit_acts(u, nib, B8set, scales):
        tm = tmp[u % 2]
        eu = ext_u8[u % 2]
        msrc = eu[0] if nib == 0 else eu[1]
        (r4sb, r6sb) = scales
        wta = list(dq_act_wait.pop((u, nib), ()))
        em.on("scalar", lambda s, msrc=msrc, tm=tm, sc=r4sb: (
            s.activation(out=tm[1], in_=msrc, func=AF.Relu, scale=sc[0], bias=sc[1]) if sc[0] is not None
            else s.activation(out=tm[1], in_=msrc, func=AF.Relu, bias=sc[1])),
              wait=wta, tag=f"r4_{u}_{nib}")
        em.on("scalar", lambda s, msrc=msrc, tm=tm, sc=r6sb: (
            s.activation(out=tm[2], in_=msrc, func=AF.Relu, scale=sc[0], bias=sc[1]) if sc[0] is not None
            else s.activation(out=tm[2], in_=msrc, func=AF.Relu, bias=sc[1])))
        if nib == 0:
            return em.on("scalar", lambda s, tm=tm, eu=eu: s.activation(
                out=tm[3], in_=eu[2], func=AF.Sign, scale=cM1, bias=c4), inc=("AC", 1), tag=f"sg_{u}_{nib}")
        return em.on("scalar", lambda s, tm=tm, B8set=B8set: s.activation(
            out=tm[3], in_=B8set, func=AF.Sign, scale=cM1, bias=c127), inc=("AC", 1), tag=f"sg_{u}_{nib}")

    dq_act_wait = {}

    def dequant_pair(u, v, B8u, B32u, B8v, B32v, scu, scv, wouts_u, wouts_v, first_wait):
        for (uu, B32) in ((u, B32u), (v, B32v)):
            es = ext_s[uu % 2]
            wt = list(first_wait) if uu == u else []
            if ext_guard2[uu % 2]:
                wt.append(("AC", ext_guard2[uu % 2]))
            em.on("vector", lambda vv, B32=B32, d=es[0]: vv.tensor_scalar(d, B32, 0x07070707, None, A.bitwise_and),
                  wait=wt, tag=f"ext0_{uu}")
            em.on("vector", lambda vv, B32=B32, d=es[1]: vv.tensor_scalar(d, B32, 0x70707070, None, A.bitwise_and))
            t_e = em.on("vector", lambda vv, B32=B32, d=es[2]: vv.tensor_scalar(d, B32, 0x08080808, None, A.bitwise_and),
                        inc=("DV", 1), tag=f"ext2_{uu}")
            ext_done[uu % 2] = t_e
        last_w = 0
        for nib in range(2):
            acts = {}
            for uu, B8, sc in ((u, B8u, scu), (v, B8v, scv)):
                wta = [("DV", ext_done[uu % 2])]
                if nib == 1:
                    wta.append(("DV", w_of[(uu, 0)]))
                elif tmpset_guard[uu % 2]:
                    wta.append(("DV", tmpset_guard[uu % 2]))
                dq_act_wait[(uu, nib)] = wta
                acts[uu] = emit_acts(uu, nib, B8, LO_SC if nib == 0 else HI_SC)
            tvals = {}
            for uu, sc_sl in ((u, scu_sl[0]), (v, scu_sl[1])):
                tm = tmp[uu % 2]
                wtb = [("AC", acts[uu])]
                if uu == u and nib == 0:
                    if st_dq["scl"]:
                        wtb.append(("LDS", st_dq["scl"]))
                    if st_dq["valwait"] is not None:
                        wtb.append(("PE", st_dq["valwait"]))
                        st_dq["valwait"] = None
                tvals[(uu, "A")] = em.on("vector", lambda vv, tm=tm: vv.scalar_tensor_tensor(
                    tm[1], tm[2], 2.0, tm[1], A.mult, A.add), wait=wtb, inc=("DV", 1), tag=f"A_{uu}_{nib}")
                tvals[(uu, "fs")] = em.on("vector", lambda vv, tm=tm, s=sc_sl: vv.scalar_tensor_tensor(
                    tm[3], tm[3], 0.5, s, A.mult, A.mult), inc=("DV", 1), tag=f"fs_{uu}_{nib}")
            for uu in (u, v):
                tm = tmp[uu % 2]
                eu = ext_u8[uu % 2]
                msrc = eu[0] if nib == 0 else eu[1]
                kmul = 1.0 if nib == 0 else 1.0 / 16
                tvals[(uu, "mag")] = em.on("vector", lambda vv, tm=tm, msrc=msrc, kmul=kmul: vv.scalar_tensor_tensor(
                    tm[1], msrc, kmul, tm[1], A.mult, A.add),
                    wait=[("DV", tvals[(uu, "A")])], inc=("DV", 1), tag=f"mag_{uu}_{nib}")
            for uu, wouts in ((u, wouts_u), (v, wouts_v)):
                tm = tmp[uu % 2]
                wout = wouts[nib]
                last_w = em.on("vector", lambda vv, tm=tm, wout=wout: vv.tensor_tensor(wout, tm[1], tm[3], A.mult),
                               wait=[("DV", tvals[(uu, "mag")]), ("DV", tvals[(uu, "fs")])],
                               inc=("DV", 1), tag=f"w_{uu}_{nib}")
                w_of[(uu, nib)] = last_w
                tmpset_guard[uu % 2] = last_w
        return last_w

    ext_done = [0, 0]
    ext_guard2 = [0, 0]
    w_of = {}
    scu_sl = [None, None]

    interA_guard = 0
    interB_guard = 0
    psg_guard = 0
    psu_guard = 0
    uglobal = 0

    def emit_deq_B_dmas(ch):
        nonlocal scl_guard
        p = ch % 2
        bsem = "LDB0" if p == 0 else "LDB1"
        bw = [("DV", extr_guard[p])] if extr_guard[p] else [("PE", pe_a5)]
        em.dma("sync", R3u8[:, p * 8192: p * 8192 + 4096],
               bass.AP(gby_d.tensor, ch * 512, [[MLP, 128], [128 * MLP, 8], [1, 512]]),
               sem=bsem, wait=bw)
        t_b = em.dma("sync", R3u8[:, p * 8192 + 4096: p * 8192 + 8192],
                     bass.AP(uby_d.tensor, ch * 512, [[MLP, 128], [128 * MLP, 8], [1, 512]]),
                     sem=bsem, wait=())
        sw = [("DV", scl_guard)] if scl_guard else [("PE", pe_a5)]
        t_s = None
        for mi2, sd in enumerate((gsc_d, usc_d)):
            t_s = em.dma("gpsimd", R4[:, mi2 * 4096: (mi2 + 1) * 4096],
                         bass.AP(sd.tensor, ch * 512, [[MLP, 128], [128 * MLP, 8], [1, 512]]),
                         sem="LDS", wait=sw if mi2 == 0 else ())
        st_dq["scl"] = t_s
        st_dq["valwait"] = valset_guard[p]
        return (p, t_b, bsem)

    def emit_deq_B_pair(ch, k, dmainfo, t_deq_mi):
        nonlocal uglobal, scl_guard
        p, t_b, bsem = dmainfo
        mi, gp = k // 2, k % 2
        base_b = p * 8192 + mi * 4096
        vset_off = p * 16384 + mi * 8192
        prs = []
        for gr in (2 * gp, 2 * gp + 1):
            B8 = R3u8[:, base_b + gr * 1024: base_b + (gr + 1) * 1024]
            B32 = R3u8.bitcast(U32)[:, (base_b + gr * 1024) // 4: (base_b + (gr + 1) * 1024) // 4]
            sc_sl = R4[:, mi * 4096 + gr * 1024: mi * 4096 + (gr + 1) * 1024]
            wl = bass.AP(R2, vset_off + (4 * gr) * 512, [[32768, 128], [1024, 2], [1, 512]])
            wh = bass.AP(R2, vset_off + (4 * gr + 1) * 512, [[32768, 128], [1024, 2], [1, 512]])
            prs.append((B8, B32, sc_sl, (wl, wh)))
        scu_sl[0], scu_sl[1] = prs[0][2], prs[1][2]
        fw = [(bsem, t_b)] if k == 0 else []
        t_w = dequant_pair(uglobal, uglobal + 1, prs[0][0], prs[0][1], prs[1][0], prs[1][1],
                           prs[0][2], prs[1][2], prs[0][3], prs[1][3], fw)
        uglobal += 2
        if k == 1:
            t_deq_mi[0] = t_w
        if k == 3:
            t_deq_mi[1] = t_w
            extr_guard[p] = t_w
            scl_guard = t_w

    bstate = {"interA": 0, "interB": 0, "psg": 0, "psu": 0}

    def emit_compute_B_nt(ch, p, t_deq_mi, nt):
        if True:
            blk, parn = nt // 2, nt % 2
            r = ch * 4 + nt
            t_mm_gate = 0
            for mi in range(2):
                pst = ps_g if mi == 0 else ps_u
                vset_off = p * 16384 + mi * 8192
                for tc in range(4):
                    for kk in range(16):
                        lhs = R2b[:, vset_off + kk * 512 + blk * 256 + parn:
                                  vset_off + kk * 512 + (blk + 1) * 256: 2]
                        rhs = xfT[:, kk * T + tc * 512: kk * T + (tc + 1) * 512]
                        wt = []
                        if tc == 0 and kk == 0:
                            if nt == 0:
                                wt.append(("DV", t_deq_mi[mi]))
                                if ch == 0 and mi == 0:
                                    wt.append(("DV", dv_a5))
                            if mi == 0 and bstate["psg"]:
                                wt.append(("AC", bstate["psg"]))
                            if mi == 1 and bstate["psu"]:
                                wt.append(("DV", bstate["psu"]))
                        inc = ("PE", 1) if (tc == 3 and kk == 15) else None
                        em.on("tensor", lambda t, pst=pst, tc=tc, lhs=lhs, rhs=rhs, kk=kk: t.matmul(
                            pst[:, tc * 512:(tc + 1) * 512], lhs, rhs, start=(kk == 0), stop=(kk == 15)),
                              wait=wt, inc=inc, tag=f"ffn_{ch}_{nt}_{mi}_{tc}_{kk}")
                if mi == 0:
                    t_mm_gate = em.cnt["PE"]
            t_mm = em.cnt["PE"]
            wtg = [("PE", t_mm_gate)]
            if bstate["interA"]:
                wtg.append(("DV", bstate["interA"]))
            t_gelu = em.on("scalar", lambda s: s.activation(out=interA, in_=ps_g, func=AF.Gelu_apprx_tanh),
                           wait=wtg, inc=("AC", 1), tag=f"gelu_{ch}_{nt}")
            bstate["psg"] = t_gelu
            wtm2 = [("AC", t_gelu)]
            if bstate["interB"]:
                wtm2.append(("STI", bstate["interB"]))
            wtm2.append(("PE", t_mm))
            t_mult = em.on("vector", lambda v: v.tensor_tensor(interB, interA, ps_u, A.mult),
                           wait=wtm2, inc=("DV", 1), tag=f"imul_{ch}_{nt}")
            bstate["interA"] = t_mult
            bstate["psu"] = t_mult
            bstate["interB"] = em.dma("sync", intT_h[r * 128:(r + 1) * 128, :], interB, sem="STI",
                                      wait=[("DV", t_mult)])
        if nt == 3:
            valset_guard[p] = em.cnt["PE"]

    bst = []
    for ch in range(33):
        dmainfo = None
        if ch < 32:
            dmainfo = emit_deq_B_dmas(ch)
            bst.append((dmainfo, [0, 0]))
        for k in range(4):
            if ch < 32:
                emit_deq_B_pair(ch, k, bst[ch][0], bst[ch][1])
            if ch >= 1:
                emit_compute_B_nt(ch - 1, bst[ch - 1][0][0], bst[ch - 1][1], k)
    pe_b = em.cnt["PE"]
    sti_total = em.cnt["STI"]

    if upto == "B":
        em.wait("sync", [("STI", sti_total), ("DV", em.cnt["DV"]), ("AC", em.cnt["AC"]),
                         ("PE", em.cnt["PE"]), ("ST", em.cnt["ST"])])
        if "inter" in debug:
            em.dma("sync", dbg_out("inter", [MLP, T], BF16), intT_h, sem="ST",
                   wait=[("STI", sti_total)])
            em.wait("sync", [("ST", em.cnt["ST"])])
        _replay(nc, em)
        for g in reversed(ctx):
            g.__exit__(None, None, None)
        return nc, dbg

    accF = R2[:].bitcast(F32)
    WdVh = [R1[:, 0:16384], R1[:, 16384:32768]]
    stgB = [scrv(22528, 1024, F32), scrv(26624, 1024, F32)]
    ist = [scrv(30720, 2048, BF16), scrv(34816, 2048, BF16)]
    ps_c = [PS[:, 0:2048], PS[:, 2048:4096]]
    ac_b = bstate["psg"]
    dv_b = bstate["psu"]

    valset_guard_c = [pe_b, pe_b]
    ist_guard = [0, 0]
    psd_guard = [None, None]
    stb_guard = [0, 0]
    first_ci = [True, True]

    def emit_deq_C(hs):
        nonlocal scl_guard
        p = hs % 2
        bsem = "LDB0" if p == 0 else "LDB1"
        t_b = em.dma("sync", R3u8[:, p * 8192:(p + 1) * 8192],
                     bass.AP(dby_d.tensor, hs * 512 * H,
                             [[H, 128], [128 * H, 4], [1, 2048]]),
                     sem=bsem, wait=[("DV", extr_guard[p])])
        t_s = em.dma("gpsimd", R4[:],
                     bass.AP(dsc_d.tensor, hs * 512 * H,
                             [[H, 128], [128 * H, 4], [1, 2048]]),
                     sem="LDS", wait=[("DV", scl_guard)])
        st_dq["scl"] = t_s
        st_dq["valwait"] = valset_guard_c[p]
        return (p, t_b, bsem)

    def emit_deq_C_pair(hs, gp, dmainfo, tws):
        nonlocal uglobal, scl_guard
        p, t_b, bsem = dmainfo
        kt = gp
        prs = []
        for hc in range(2):
            base = p * 8192 + kt * 2048 + hc * 1024
            B8 = R3u8[:, base: base + 1024]
            B32 = R3u8.bitcast(U32)[:, base // 4: base // 4 + 256]
            sc_sl = R4[:, kt * 2048 + hc * 1024: kt * 2048 + (hc + 1) * 1024]
            wl = WdVh[p][:, (2 * kt) * 2048 + hc * 1024: (2 * kt) * 2048 + (hc + 1) * 1024]
            wh = WdVh[p][:, (2 * kt + 1) * 2048 + hc * 1024: (2 * kt + 1) * 2048 + (hc + 1) * 1024]
            prs.append((B8, B32, sc_sl, (wl, wh)))
        scu_sl[0], scu_sl[1] = prs[0][2], prs[1][2]
        fw = [(bsem, t_b)] if gp == 0 else []
        t_w = dequant_pair(uglobal, uglobal + 1, prs[0][0], prs[0][1], prs[1][0], prs[1][1],
                           prs[0][2], prs[1][2], prs[0][3], prs[1][3], fw)
        uglobal += 2
        if gp == 3:
            extr_guard[p] = t_w
            scl_guard = t_w
        tws[gp] = t_w

    def emit_compute_C_tp(hs, p, t_w, tp2):
        tpp = tp2 % 2
        isem = "LDI0" if tpp == 0 else "LDI1"
        jsem = "LDJ0" if tpp == 0 else "LDJ1"
        iw = [("PE", ist_guard[tpp])] if ist_guard[tpp] else []
        if first_ci[tpp]:
            iw.append(("STI", sti_total))
            first_ci[tpp] = False
        t_i = em.dma("sync", ist[tpp][:, 0:1024],
                     bass.AP(intT_h.tensor, (hs * 8) * 128 * T + tp2 * 256,
                             [[T, 128], [128 * T, 4], [1, 256]]),
                     sem=isem, wait=iw)
        t_j = em.dma("gpsimd", ist[tpp][:, 1024:2048],
                     bass.AP(intT_h.tensor, (hs * 8 + 4) * 128 * T + tp2 * 256,
                             [[T, 128], [128 * T, 4], [1, 256]]),
                     sem=jsem, wait=iw)
        t_mm = 0
        for sub in range(2):
            tt = tp2 * 2 + sub
            tpd = tt % 2
            for i in range(8):
                for h2 in range(4):
                    wt = []
                    if i == 0 and h2 == 0:
                        wt = [(isem, t_i), (jsem, t_j), ("DV", t_w)]
                        if psd_guard[tpd] is not None:
                            wt.append(("DV", psd_guard[tpd]))
                        elif hs == 0:
                            wt.extend([("AC", ac_b), ("DV", dv_b)])
                    inc = ("PE", 1) if (i == 7 and h2 == 3) else None
                    em.on("tensor", lambda t, tpd=tpd, i=i, h2=h2, p=p, tpp=tpp, sub=sub: t.matmul(
                        ps_c[tpd][:, h2 * 512:(h2 + 1) * 512],
                        ist[tpp][:, i * 256 + sub * 128: i * 256 + sub * 128 + 128],
                        WdVh[p][:, i * 2048 + h2 * 512: i * 2048 + h2 * 512 + 512],
                        start=(i == 0), stop=(i == 7)), wait=wt, inc=inc,
                          tag=f"dn_{hs}_{tt}_{i}_{h2}")
            t_mm = em.cnt["PE"]
            acc_t = accF[:, tt * 1024:(tt + 1) * 1024]
            if hs == 0:
                t_ev = em.on("vector", lambda v, acc_t=acc_t, tpd=tpd: v.tensor_copy(
                    acc_t, ps_c[tpd][:, 0:1024]),
                             wait=[("PE", t_mm), ("PE", pe_b)], inc=("DV", 1), tag=f"acc0_{tt}")
            else:
                t_ev = em.on("vector", lambda v, acc_t=acc_t, tpd=tpd: v.tensor_tensor(
                    acc_t, acc_t, ps_c[tpd][:, 0:1024], A.add), wait=[("PE", t_mm)], inc=("DV", 1),
                      tag=f"acc_{hs}_{tt}")
            wtb2 = [("PE", t_mm)]
            if stb_guard[tpd]:
                wtb2.append(("STB0" if tpd == 0 else "STB1", stb_guard[tpd]))
            t_cpB = em.on("vector", lambda v, tpd=tpd: v.tensor_copy(stgB[tpd], ps_c[tpd][:, 1024:2048]),
                          wait=wtb2, inc=("DV", 1), tag=f"cpB_{hs}_{tt}")
            stb_guard[tpd] = em.dma("gpsimd", out_d[tt * 128:(tt + 1) * 128, 1024:2048], stgB[tpd],
                                    sem="STB0" if tpd == 0 else "STB1", wait=[("DV", t_cpB)],
                                    **({"accum_op": A.add} if hs > 0 else {}))
            if hs == 15:
                em.dma("sync", out_d[tt * 128:(tt + 1) * 128, 0:1024], acc_t,
                       sem="STO0" if tpd == 0 else "STO1", wait=[("DV", t_ev)])
            psd_guard[tpd] = t_cpB
        ist_guard[tpp] = t_mm
        if tp2 == 7:
            valset_guard_c[p] = em.cnt["PE"]

    cst = []
    for s in range(17):
        if s < 16:
            cst.append([s, emit_deq_C(s), [0, 0, 0, 0]])
        for k in range(8):
            if s < 16 and k < 4:
                emit_deq_C_pair(cst[s][0], k, cst[s][1], cst[s][2])
            if s >= 1:
                phs, pdma, ptws = cst[s - 1]
                emit_compute_C_tp(phs, pdma[0], ptws[3], k)

    x2f = [scrv(6144, 2048, F32), scrv(22528, 2048, F32)]
    outf = [scrv(14336, 2048, F32), scrv(30720, 2048, F32)]
    sto_all = [("STO0", em.cnt["STO0"]), ("STO1", em.cnt["STO1"]),
               ("STB0", em.cnt["STB0"]), ("STB1", em.cnt["STB1"])]
    fin_guard = [0, 0]
    fin_st = [0, 0]
    for tt in range(NT):
        fp_ = tt % 2
        lsem = "LDX0" if fp_ == 0 else "LDX1"
        osem = "LDI0" if fp_ == 0 else "LDI1"
        fsem = "STF0" if fp_ == 0 else "STF1"
        g = [("DV", fin_guard[fp_]), (fsem, fin_st[fp_])] if fin_guard[fp_] else (sto_all + [("DV", em.cnt["DV"])])
        t_xl = em.dma("sync", x2f[fp_], x2_h[tt * 128:(tt + 1) * 128, :], sem=lsem, wait=g)
        t_ol = em.dma("sync", outf[fp_], out_d[tt * 128:(tt + 1) * 128, :], sem=osem, wait=g)
        t_ad = em.on("vector", lambda v, fp_=fp_: v.tensor_tensor(outf[fp_], outf[fp_], x2f[fp_], A.add),
                     wait=[(lsem, t_xl), (osem, t_ol)], inc=("DV", 1), tag=f"fin_{tt}")
        fin_guard[fp_] = t_ad
        fin_st[fp_] = em.dma("sync", out_d[tt * 128:(tt + 1) * 128, :], outf[fp_], sem=fsem,
                             wait=[("DV", t_ad)])

    em.wait("sync", [("ST", em.cnt["ST"]), ("STF0", em.cnt["STF0"]), ("STF1", em.cnt["STF1"]),
                     ("STO0", em.cnt["STO0"]), ("STO1", em.cnt["STO1"]),
                     ("DV", em.cnt["DV"]), ("AC", em.cnt["AC"]),
                     ("PE", em.cnt["PE"]), ("LD2", em.cnt["LD2"]), ("LDS", em.cnt["LDS"]),
                     ("STI", em.cnt["STI"]), ("LDI0", em.cnt["LDI0"]), ("LDI1", em.cnt["LDI1"])])

    _replay(nc, em)
    for g in reversed(ctx):
        g.__exit__(None, None, None)
    return nc, dbg


def _replay(nc, em):
    sems = {}
    semctx = []
    for name in sorted(em.cnt.keys()):
        g = nc.semaphore(name)
        sems[name] = g.__enter__()
        semctx.append(g)
    with nc.Block() as block:
        def runner(eng_obj, key):
            for fn, waits, inc, tag in em.ops[key]:
                seen = {}
                for s, val in waits:
                    if val > 0 and seen.get(s, -1) < val:
                        eng_obj.wait_ge(sems[s], val)
                        seen[s] = val
                if fn is None:
                    continue
                if fn == "DRAIN":
                    eng_obj.drain()
                    continue
                r = fn(eng_obj)
                if inc is not None:
                    r.then_inc(sems[inc[0]], inc[1])
        @block.sync
        def _(e): runner(e, "sync")
        @block.vector
        def _(e): runner(e, "vector")
        @block.scalar
        def _(e): runner(e, "scalar")
        @block.tensor
        def _(e): runner(e, "tensor")
        @block.gpsimd
        def _(e): runner(e, "gpsimd")
    for g in reversed(semctx):
        g.__exit__(None, None, None)



def _host_prep(inputs):
    x = np.ascontiguousarray(np.asarray(inputs["L0_attn_hidden_states"], dtype=np.float32))
    d = {}
    d["wqT"] = np.ascontiguousarray(np.asarray(inputs["L0_attn_wq"], dtype=np.float32).T)
    d["wkT"] = np.ascontiguousarray(np.asarray(inputs["L0_attn_wk"], dtype=np.float32).T)
    d["wvT"] = np.ascontiguousarray(np.asarray(inputs["L0_attn_wv"], dtype=np.float32).T)
    d["woT"] = np.ascontiguousarray(np.asarray(inputs["L0_attn_wo"], dtype=np.float32).T)
    d["qw"] = np.ascontiguousarray(np.asarray(inputs["L0_attn_q_norm_w"], dtype=np.float32).reshape(1, -1))
    d["kw"] = np.ascontiguousarray(np.asarray(inputs["L0_attn_k_norm_w"], dtype=np.float32).reshape(1, -1))

    def bytes_T(p):
        a = np.ascontiguousarray(np.asarray(p, dtype=np.int32))
        b = a.view(np.uint8)[:, 0::4]
        return np.ascontiguousarray(b.T)

    d["gby"] = bytes_T(inputs["L0_ffn_gate_W_packed"])
    d["uby"] = bytes_T(inputs["L0_ffn_up_W_packed"])
    d["dby"] = bytes_T(inputs["L0_ffn_down_W_packed"])
    d["gsc"] = np.ascontiguousarray(np.repeat(np.asarray(inputs["L0_ffn_gate_scales"], dtype=np.float32).T, 16, axis=0))
    d["usc"] = np.ascontiguousarray(np.repeat(np.asarray(inputs["L0_ffn_up_scales"], dtype=np.float32).T, 16, axis=0))
    d["dsc"] = np.ascontiguousarray(np.repeat(np.asarray(inputs["L0_ffn_down_scales"], dtype=np.float32).T, 16, axis=0))
    d["identD"] = np.eye(128).astype(NP_BF16)
    inv_freq = (1.0 / 10000.0 ** (np.arange(0, HD, 2, dtype=np.float32) / HD)).astype(np.float32)
    tpos = np.arange(T, dtype=np.float32)
    freqs = np.outer(tpos, inv_freq).astype(np.float32)
    cos = np.cos(freqs).astype(np.float32); sin = np.sin(freqs).astype(np.float32)
    d["cosTD"] = np.ascontiguousarray(cos.reshape(NT, 128, 64).transpose(1, 0, 2).reshape(128, NT * 64))
    d["sinTD"] = np.ascontiguousarray(sin.reshape(NT, 128, 64).transpose(1, 0, 2).reshape(128, NT * 64))
    mi = np.arange(128)
    d["maskdD"] = np.where(mi[None, :] <= mi[:, None], 0.0, -1e30).astype(np.float32)
    d["epscD"] = np.full((128, 1), EPS, np.float32)
    cN = np.zeros((128, 12), np.float32)
    for i, vv in enumerate([0.5, -2.0, -6.0, 1.0 / 32, 1.0 / 16, -1.0, 4.0, 127.5, SM_SCALE, -4.0]):
        cN[:, i] = vv
    d["consND"] = cN
    return x, d


def run_dev(inputs, debug=(), cores=(0,), upto="ALL", nchunks=None, nosq=False):
    x, shared = _host_prep(inputs)
    nc, dbg = build(debug=debug, upto=upto, nchunks=nchunks, nosq=nosq)
    in_maps = []
    for c in cores:
        m = dict(shared)
        m["x"] = np.ascontiguousarray(x[c])
        in_maps.append(m)
    return run_bass_kernel_spmd(nc, in_maps, core_ids=list(cores))


def kernel(**inputs):
    x, shared = _host_prep(inputs)
    nc, _ = build()
    in_maps = []
    for c in range(8):
        m = dict(shared)
        m["x"] = np.ascontiguousarray(x[c])
        in_maps.append(m)
    res = run_bass_kernel_spmd(nc, in_maps, core_ids=list(range(8)))
    return np.stack([np.asarray(r["out"]) for r in res.results]).astype(np.float32)


# version 16
# speedup vs baseline: 1.1164x; 1.1164x over previous
import numpy as np
import ml_dtypes
import concourse.bass as bass
import concourse.mybir as mybir
from concourse.bass_utils import run_bass_kernel_spmd

F32 = mybir.dt.float32
BF16 = mybir.dt.bfloat16
U8 = mybir.dt.uint8
U32 = mybir.dt.uint32
A = mybir.AluOpType
AF = mybir.ActivationFunctionType
AX = mybir.AxisListType
NP_BF16 = ml_dtypes.bfloat16

T = 2048
H = 2048
NH, NKV, HD = 16, 4, 128
MLP = 16384
NT = T // 128
NHC = H // 128
EPS = 1e-5
SM_SCALE = float(1.0 / np.sqrt(HD))

ENGINES = ("sync", "vector", "scalar", "tensor", "gpsimd")
NODRAIN = False
NOACW = False
NTT = None


class Emitter:
    def __init__(self):
        self.ops = {k: [] for k in ENGINES}
        self.cnt = {}
        for s in ("LDC", "LD2", "LDR", "LDX0", "LDX1", "LDW0", "LDW1", "PE", "DV", "AC", "PO", "ST", "STX0", "STX1", "LDB0", "LDB1", "LDS", "STI", "LDI0", "LDI1", "STO0", "STO1", "STF0", "STF1", "LDJ0", "LDJ1"):
            self.cnt[s] = 0

    def on(self, eng, fn=None, wait=(), inc=None, tag=""):
        ret = None
        if inc is not None:
            assert fn is not None, f"inc without instruction at {tag}"
            self.cnt[inc[0]] += inc[1]
            ret = self.cnt[inc[0]]
        self.ops[eng].append((fn, tuple(wait), inc, tag))
        return ret

    def wait(self, eng, wait):
        self.ops[eng].append((None, tuple(wait), None, "wait"))

    def drain(self, eng):
        self.ops[eng].append(("DRAIN", (), None, "drain"))

    def dma(self, queue, out_ap, in_ap, sem="LD", wait=(), **kw):
        fn = lambda e, o=out_ap, i=in_ap, kw=kw: e.dma_start(out=o, in_=i, **kw)
        return self.on(queue, fn, wait=wait, inc=(sem, 16))


def build(debug=(), upto="ALL", nchunks=None, nosq=False):
    nc = bass.Bass("TRN2", target_bir_lowering=False)
    em = Emitter()

    x_d = nc.dram_tensor("x", [T, H], F32, kind="ExternalInput").ap()
    wqT_d = nc.dram_tensor("wqT", [H, NH * HD], F32, kind="ExternalInput").ap()
    wkT_d = nc.dram_tensor("wkT", [H, NKV * HD], F32, kind="ExternalInput").ap()
    wvT_d = nc.dram_tensor("wvT", [H, NKV * HD], F32, kind="ExternalInput").ap()
    woT_d = nc.dram_tensor("woT", [NH * HD, H], F32, kind="ExternalInput").ap()
    qw_d = nc.dram_tensor("qw", [1, NH * HD], F32, kind="ExternalInput").ap()
    kw_d = nc.dram_tensor("kw", [1, NKV * HD], F32, kind="ExternalInput").ap()
    gby_d = nc.dram_tensor("gby", [H // 2, MLP], U8, kind="ExternalInput").ap()
    uby_d = nc.dram_tensor("uby", [H // 2, MLP], U8, kind="ExternalInput").ap()
    dby_d = nc.dram_tensor("dby", [MLP // 2, H], U8, kind="ExternalInput").ap()
    gsc_d = nc.dram_tensor("gsc", [H // 2, MLP], F32, kind="ExternalInput").ap()
    usc_d = nc.dram_tensor("usc", [H // 2, MLP], F32, kind="ExternalInput").ap()
    dsc_d = nc.dram_tensor("dsc", [MLP // 2, H], F32, kind="ExternalInput").ap()
    ident_d = nc.dram_tensor("identD", [128, 128], BF16, kind="ExternalInput").ap()
    cosT_d = nc.dram_tensor("cosTD", [128, NT * 64], F32, kind="ExternalInput").ap()
    sinT_d = nc.dram_tensor("sinTD", [128, NT * 64], F32, kind="ExternalInput").ap()
    mask_d = nc.dram_tensor("maskdD", [128, 128], F32, kind="ExternalInput").ap()
    epsc_d = nc.dram_tensor("epscD", [128, 1], F32, kind="ExternalInput").ap()
    consN_d = nc.dram_tensor("consND", [128, 12], F32, kind="ExternalInput").ap()

    x2_h = nc.dram_tensor("x2h", [T, H], F32, kind="Internal").ap()
    intT_h = nc.dram_tensor("intTh", [MLP, T], BF16, kind="Internal").ap()
    out_d = nc.dram_tensor("out", [T, H], F32, kind="ExternalOutput").ap()

    dbg = {}
    def dbg_out(name, shape, dtype):
        ap = nc.dram_tensor("dbg_" + name, shape, dtype, kind="ExternalOutput").ap()
        dbg[name] = ap
        return ap

    ctx = []
    def sbuf(name, shape, dtype):
        g = nc.sbuf_tensor(name, shape, dtype)
        t = g.__enter__(); ctx.append(g); return t
    def psumt(name, shape, dtype=F32):
        g = nc.psum_tensor(name, shape, dtype)
        t = g.__enter__(); ctx.append(g); return t

    ident = sbuf("ident", [128, 128], BF16)
    maskd = sbuf("maskd", [128, 128], F32)
    epsc = sbuf("epsc", [128, 1], F32)
    consN = sbuf("consN", [128, 12], F32)
    qw_e = sbuf("qw_e", [128, NH * HD], BF16)
    kw_e = sbuf("kw_e", [128, NKV * HD], BF16)
    stat = sbuf("stat", [128, 16], F32)
    qk_sums = sbuf("qk_sums", [128, NT * 12], F32)

    R1 = sbuf("R1", [128, 32768], BF16)
    R2 = sbuf("R2", [128, 32768], BF16)
    R3 = sbuf("R3", [128, 8192], BF16)
    R4 = sbuf("R4", [128, 8192], BF16)
    scr = sbuf("scr", [128, 40960], U8)

    def scrv(off_bytes, n, dtype):
        b = scr[:].bitcast(dtype)
        esz = {F32: 4, BF16: 2, U8: 1, U32: 4}[dtype]
        assert off_bytes % esz == 0 and off_bytes + n * esz <= 40960
        return b[:, off_bytes // esz: off_bytes // esz + n]

    xlnT = R1[:]
    q_sb = R2[:]
    v_sb = R4[:]

    xs = [scrv(0, 2048, F32), scrv(8192, 2048, F32)]
    kT = scrv(0, 8192, BF16)
    k_st = R3[:]
    qn_s = scrv(16384, 2048, BF16)
    rp_a = scrv(20480, 1024, BF16)
    rp_b = scrv(22528, 1024, BF16)
    xln_s = scrv(24576, 2048, BF16)
    wst = [scrv(16384, 4096, BF16), scrv(28672, 4096, BF16)]
    cosr = scrv(28672, 1024, F32)
    sinr = scrv(32768, 1024, F32)

    PS = psumt("PS", [128, 4096], F32)
    def ps_bank(i, n=1):
        return PS[:, i * 512:(i + n) * 512]
    ps_tp = [ps_bank(0).bitcast(BF16)[:, 0:128], ps_bank(1).bitcast(BF16)[:, 0:128]]
    ps_q = ps_bank(2)[:, 0:256]
    ps_k = ps_bank(3)[:, 0:256]
    ps_v = ps_bank(4)[:, 0:256]

    tp_guard = [0, 0]
    xs_guard = [0, 0]
    ps_guard = {"q": [], "k": [], "v": []}

    em.dma("sync", ident[:], ident_d, sem="LDC")
    em.dma("sync", maskd[:], mask_d, sem="LDC")
    em.dma("sync", epsc[:], epsc_d, sem="LDC")
    em.dma("sync", consN[:], consN_d, sem="LDC")
    em.dma("gpsimd", qw_e[:], bass.AP(qw_d.tensor, 0, [[0, 128], [1, NH * HD]]), sem="LD2")
    em.dma("gpsimd", kw_e[:], bass.AP(kw_d.tensor, 0, [[0, 128], [1, NKV * HD]]), sem="LD2")
    c_ld = em.cnt["LDC"]
    c_ld2 = em.cnt["LD2"]

    xr_guard = [0, 0]
    sq_prev = 0
    for tt in range(NT):
        ttp = tt % 2
        sb0 = 0 if ttp == 0 else 8
        xr = R3[:, ttp * 2048:(ttp + 1) * 2048]
        dump = R3[:, 4096 + ttp * 2048: 6144 + ttp * 2048]
        xb = xs[ttp]
        lsem = "LDX0" if ttp == 0 else "LDX1"
        t_x = em.dma("sync", xb, x_d[tt * 128:(tt + 1) * 128, :], sem=lsem,
                     wait=[("DV", xs_guard[ttp])] if xs_guard[ttp] else ())
        t_red = em.on("vector", lambda v, xb=xb, sb0=sb0: v.tensor_reduce(stat[:, sb0:sb0 + 1], xb, AX.X, A.add),
              wait=[(lsem, t_x)], inc=("DV", 1), tag=f"ln1sum{tt}")
        t_sq = em.on("scalar", lambda s, xb=xb, dump=dump, sb0=sb0: s.activation(
            out=dump, in_=xb, func=AF.Square, accum_out=stat[:, sb0 + 1:sb0 + 2]),
                     wait=[(lsem, t_x)] + ([("AC", sq_prev)] if sq_prev else []),
                     inc=("AC", 1), tag=f"ln1sq{tt}")
        sq_prev = t_sq
        t_mean = em.on("vector", lambda v, sb0=sb0: v.tensor_scalar(
            stat[:, sb0 + 2:sb0 + 3], stat[:, sb0:sb0 + 1], 1.0 / H, None, A.mult),
            wait=[("DV", t_red)], inc=("DV", 1))
        t_mm2 = em.on("vector", lambda v, sb0=sb0: v.tensor_tensor(
            stat[:, sb0 + 3:sb0 + 4], stat[:, sb0 + 2:sb0 + 3], stat[:, sb0 + 2:sb0 + 3], A.mult),
            wait=[("DV", t_mean)], inc=("DV", 1))
        t_var = em.on("vector", lambda v, sb0=sb0: v.scalar_tensor_tensor(
            stat[:, sb0 + 4:sb0 + 5], stat[:, sb0 + 1:sb0 + 2], 1.0 / H,
            stat[:, sb0 + 3:sb0 + 4], A.mult, A.subtract),
                      wait=[("AC", t_sq), ("DV", t_mm2)], inc=("DV", 1))
        t_sd = em.on("scalar", lambda s, sb0=sb0: s.activation(
            out=stat[:, sb0 + 5:sb0 + 6], in_=stat[:, sb0 + 4:sb0 + 5], func=AF.Sqrt, bias=epsc[:]),
                     wait=[("DV", t_var), ("LDC", c_ld)], inc=("AC", 1))
        t_rc = em.on("vector", lambda v, sb0=sb0: v.reciprocal(
            stat[:, sb0 + 6:sb0 + 7], stat[:, sb0 + 5:sb0 + 6]), wait=[("AC", t_sd)], inc=("DV", 1))
        wxl = [("DV", t_rc)]
        if xr_guard[ttp]:
            wxl.append(("PE", xr_guard[ttp]))
        t_xln = em.on("vector", lambda v, xb=xb, xr=xr, sb0=sb0: v.tensor_scalar(
            xr, xb, stat[:, sb0 + 2:sb0 + 3], stat[:, sb0 + 6:sb0 + 7], A.subtract, A.mult),
            wait=wxl, inc=("DV", 1))
        xs_guard[ttp] = t_xln
        for hc in range(NHC):
            pt = ps_tp[hc % 2]
            src = xr[:, hc * 128:(hc + 1) * 128]
            dst = xlnT[:, hc * T + tt * 128: hc * T + (tt + 1) * 128]
            wt = [("DV", t_xln)]
            if tt == 0 and hc < 2:
                wt.append(("LDC", c_ld))
            if tp_guard[hc % 2]:
                wt.append(("DV", tp_guard[hc % 2]))
            t_pe = em.on("tensor", lambda t, pt=pt, src=src: t.transpose(pt, src, ident[:]),
                         wait=wt, inc=("PE", 1), tag=f"tpA1_{tt}_{hc}")
            tp_guard[hc % 2] = em.on("vector", lambda v, pt=pt, dst=dst: v.tensor_copy(dst, pt),
                                     wait=[("PE", t_pe)], inc=("DV", 1), tag=f"cpA1_{tt}_{hc}")
        xr_guard[ttp] = em.cnt["PE"]
    dv_a1 = em.cnt["DV"]

    if "xlnT" in debug:
        em.dma("sync", dbg_out("xlnT", [128, NHC * T], BF16), xlnT, sem="ST", wait=[("DV", dv_a1)])

    if upto == "A1":
        em.wait("sync", [("ST", em.cnt["ST"]), ("DV", em.cnt["DV"]), ("AC", em.cnt["AC"]),
                         ("PE", em.cnt["PE"]), ("LD2", em.cnt["LD2"])])
        _replay(nc, em)
        for g in reversed(ctx):
            g.__exit__(None, None, None)
        return nc, dbg

    CHUNKS = ([("q", oc, wqT_d, NH * HD, oc) for oc in range(8)]
              + [("k", oc, wkT_d, NKV * HD, 8 + oc) for oc in range(2)]
              + [("v", oc, wvT_d, NKV * HD, None) for oc in range(2)])
    if nchunks is not None:
        CHUNKS = CHUNKS[:nchunks]
    wst_guard = [(), ()]
    for ci, (which, oc, w_dram, w_cols, sumslot) in enumerate(CHUNKS):
        wb = wst[ci % 2]
        wsem = "LDW0" if ci % 2 == 0 else "LDW1"
        t_w = em.dma("gpsimd", wb,
                     bass.AP(w_dram.tensor, oc * 256, [[w_cols, 128], [128 * w_cols, NHC], [1, 256]]),
                     sem=wsem, wait=wst_guard[ci % 2])
        ps = {"q": ps_q, "k": ps_k, "v": ps_v}[which]
        for tt in range(NTT if NTT else NT):
            for hc in range(NHC):
                lhs = xlnT[:, hc * T + tt * 128: hc * T + (tt + 1) * 128]
                rhs = wb[:, hc * 256:(hc + 1) * 256]
                wt = []
                if hc == 0:
                    wt.append((wsem, t_w))
                    if ci == 0 and tt == 0:
                        wt.append(("DV", dv_a1))
                    wt.extend(ps_guard[which])
                inc = ("PE", 1) if hc == NHC - 1 else None
                em.on("tensor", lambda t, lhs=lhs, rhs=rhs, hc=hc, ps=ps: t.matmul(
                    ps, lhs, rhs, start=(hc == 0), stop=(hc == NHC - 1)),
                      wait=wt, inc=inc, tag=f"mm_{which}{oc}_{tt}_{hc}")
            t_mm = em.cnt["PE"]
            if which == "q":
                dst = q_sb[:, tt * 2048 + oc * 256: tt * 2048 + oc * 256 + 256]
            elif which == "k":
                dst = k_st[:, tt * 512 + oc * 256: tt * 512 + oc * 256 + 256]
            else:
                dst = v_sb[:, tt * 512 + oc * 256: tt * 512 + oc * 256 + 256]
            t_ev = em.on("vector", lambda v, dst=dst, ps=ps: v.tensor_copy(dst, ps),
                         wait=[("PE", t_mm)], inc=("DV", 1), tag=f"ev_{which}{oc}_{tt}")
            ps_guard[which] = [("DV", t_ev)]
        wst_guard[ci % 2] = [("PE", em.cnt["PE"])]
    ac_proj = em.cnt["AC"]
    pe_qkv = em.cnt["PE"]

    if upto == "A2":
        if "v" in debug:
            em.dma("sync", dbg_out("v", [128, NT * 512], BF16), v_sb, sem="ST", wait=[("DV", em.cnt["DV"])])
        if "q_sb" in debug:
            em.dma("sync", dbg_out("q_sb", [128, NT * 2048], BF16), q_sb, sem="ST", wait=[("DV", em.cnt["DV"])])
        em.wait("sync", [("ST", em.cnt["ST"]), ("DV", em.cnt["DV"]), ("AC", em.cnt["AC"]),
                         ("PE", em.cnt["PE"]), ("LD2", em.cnt["LD2"])])
        _replay(nc, em)
        for g in reversed(ctx):
            g.__exit__(None, None, None)
        return nc, dbg

    t_cos = em.dma("sync", cosr, cosT_d, sem="LDR", wait=[("PE", pe_qkv)])
    t_sin = em.dma("sync", sinr, sinT_d, sem="LDR", wait=[("PE", pe_qkv)])

    qT = xlnT

    def rope_emit(src_ap, dst_ap, tt, nheads, tag):
        d2 = HD // 2
        s3 = src_ap.rearrange("p (h two d) -> p h two d", two=2, d=d2)
        o3 = dst_ap.rearrange("p (h two d) -> p h two d", two=2, d=d2)
        ch = cosr[:, tt * d2: (tt + 1) * d2]
        snh = sinr[:, tt * d2: (tt + 1) * d2]
        cb = ch.unsqueeze(1).unsqueeze(2).broadcast_to([128, nheads, 2, d2])
        sb_ = snh.unsqueeze(1).unsqueeze(2).broadcast_to([128, nheads, 2, d2])
        ra = rp_a[:, 0:nheads * d2].rearrange("p (h d) -> p h d", d=d2)
        rb = rp_b[:, 0:nheads * d2].rearrange("p (h d) -> p h d", d=d2)
        em.on("vector", lambda v: v.tensor_tensor(o3[:, :, 0, :], s3[:, :, 0, :], cb[:, :, 0, :], A.mult), tag=tag + "a")
        em.on("vector", lambda v: v.tensor_tensor(o3[:, :, 1, :], s3[:, :, 1, :], cb[:, :, 1, :], A.mult), tag=tag + "b")
        em.on("vector", lambda v: v.tensor_tensor(ra, s3[:, :, 1, :], sb_[:, :, 0, :], A.mult), tag=tag + "c")
        em.on("vector", lambda v: v.tensor_tensor(rb, s3[:, :, 0, :], sb_[:, :, 1, :], A.mult), tag=tag + "d")
        em.drain("vector")
        em.on("vector", lambda v: v.tensor_tensor(o3[:, :, 0, :], o3[:, :, 0, :], ra, A.subtract), tag=tag + "e")
        return em.on("vector", lambda v: v.tensor_tensor(o3[:, :, 1, :], o3[:, :, 1, :], rb, A.add),
                     inc=("DV", 1), tag=tag + "f")

    for tt in range(NT):
        qrow = q_sb[:, tt * 2048:(tt + 1) * 2048]
        krow0 = k_st[:, tt * 512:(tt + 1) * 512]
        em.on("vector", lambda v, krow0=krow0: v.tensor_tensor(qn_s[:, 0:512], krow0, krow0, A.mult),
              tag=f"sqk_{tt}")
        em.on("vector", lambda v, qrow=qrow: v.tensor_tensor(xln_s, qrow, qrow, A.mult), tag=f"sqq_{tt}")
        em.drain("vector")
        em.on("vector", lambda v: v.tensor_reduce(stat[:, 3:4], qn_s[:, 0:512], AX.X, A.add), tag=f"redk_{tt}")
        em.on("vector", lambda v: v.tensor_reduce(stat[:, 0:1], xln_s, AX.X, A.add), tag=f"redq_{tt}")
        em.drain("vector")
        t_q = em.on("vector", lambda v: v.tensor_scalar(stat[:, 0:1], stat[:, 0:1], 1.0 / (NH * HD), None,
                                                        A.mult), inc=("DV", 1))
        t_k = em.on("vector", lambda v: v.tensor_scalar(stat[:, 3:4], stat[:, 3:4],
                                                        1.0 / (NKV * HD), None, A.mult), inc=("DV", 1))
        t_qs = em.on("scalar", lambda s: s.activation(out=stat[:, 1:2], in_=stat[:, 0:1], func=AF.Sqrt,
                                                      bias=epsc[:]), wait=[("DV", t_k)], inc=("AC", 1))
        t_ks = em.on("scalar", lambda s: s.activation(out=stat[:, 4:5], in_=stat[:, 3:4], func=AF.Sqrt,
                                                      bias=epsc[:]), wait=[("DV", t_k)], inc=("AC", 1))
        em.on("vector", lambda v: v.reciprocal(stat[:, 2:3], stat[:, 1:2]), wait=[("AC", t_ks)])
        em.on("vector", lambda v: v.reciprocal(stat[:, 5:6], stat[:, 4:5]))
        em.drain("vector")
        em.on("vector", lambda v, qrow=qrow: v.tensor_scalar(qn_s, qrow, stat[:, 2:3], None, A.mult),
              tag=f"qn1_{tt}")
        em.drain("vector")
        em.on("vector", lambda v: v.tensor_tensor(qn_s, qn_s, qw_e[:], A.mult),
              wait=[("LD2", c_ld2), ("LDR", t_sin)] if tt == 0 else (), tag=f"qn2_{tt}")
        em.drain("vector")
        t_rope = rope_emit(qn_s, xln_s, tt, NH, f"ropeq_{tt}")
        for hq in range(NH):
            pt = ps_tp[hq % 2]
            src = xln_s[:, hq * 128:(hq + 1) * 128]
            dst = qT[:, hq * T + tt * 128: hq * T + (tt + 1) * 128]
            wt = [("DV", t_rope), ("DV", tp_guard[hq % 2])]
            t_pe = em.on("tensor", lambda t, pt=pt, src=src: t.transpose(pt, src, ident[:]),
                         wait=wt, inc=("PE", 1), tag=f"tpQ_{tt}_{hq}")
            tp_guard[hq % 2] = em.on("vector", lambda v, pt=pt, dst=dst: v.tensor_copy(dst, pt),
                                     wait=[("PE", t_pe)], inc=("DV", 1))
        pe_q_tp = em.cnt["PE"]
        krow = k_st[:, tt * 512:(tt + 1) * 512]
        em.on("vector", lambda v, krow=krow: v.tensor_scalar(qn_s[:, 0:512], krow, stat[:, 5:6], None, A.mult))
        em.drain("vector")
        em.on("vector", lambda v: v.tensor_tensor(qn_s[:, 0:512], qn_s[:, 0:512], kw_e[:], A.mult))
        em.drain("vector")
        em.wait("vector", [("PE", pe_q_tp)])
        t_ropek = rope_emit(qn_s[:, 0:512], xln_s[:, 0:512], tt, NKV, f"ropek_{tt}")
        for g in range(NKV):
            pt = ps_tp[g % 2]
            src = xln_s[:, g * 128:(g + 1) * 128]
            dst = kT[:, g * T + tt * 128: g * T + (tt + 1) * 128]
            t_pe = em.on("tensor", lambda t, pt=pt, src=src: t.transpose(pt, src, ident[:]),
                         wait=[("DV", t_ropek), ("DV", tp_guard[g % 2])], inc=("PE", 1), tag=f"tpK_{tt}_{g}")
            tp_guard[g % 2] = em.on("vector", lambda v, pt=pt, dst=dst: v.tensor_copy(dst, pt),
                                    wait=[("PE", t_pe)], inc=("DV", 1))
        em.wait("vector", [("PE", em.cnt["PE"])])


    if upto == "A2b":
        if "qT" in debug:
            em.dma("sync", dbg_out("qT", [128, NH * T], BF16), qT, sem="ST", wait=[("DV", em.cnt["DV"])])
        if "kT" in debug:
            em.dma("sync", dbg_out("kT", [128, NKV * T], BF16), kT, sem="ST", wait=[("DV", em.cnt["DV"])])
        if "v" in debug:
            em.dma("sync", dbg_out("v", [128, NT * 512], BF16), v_sb, sem="ST", wait=[("DV", em.cnt["DV"])])
        em.wait("sync", [("ST", em.cnt["ST"]), ("DV", em.cnt["DV"]), ("AC", em.cnt["AC"]),
                         ("PE", em.cnt["PE"]), ("LD2", em.cnt["LD2"])])
        _replay(nc, em)
        for g in reversed(ctx):
            g.__exit__(None, None, None)
        return nc, dbg

    dv_a2b = em.cnt["DV"]
    attT = q_sb
    P_rows = [xln_s, scrv(28672, 2048, BF16)]
    pstage = [scrv(16384 + 256 * i, 128, BF16) for i in range(4)]
    ostage = scrv(17408, 128, BF16)
    ps_sc = PS[:, 0:2048]
    ps_pt2 = [ps_bank(4).bitcast(BF16)[:, 0:128], ps_bank(5).bitcast(BF16)[:, 0:128]]
    ps_oa2 = [PS[:, 6 * 512: 6 * 512 + 128], PS[:, 6 * 512 + 128: 6 * 512 + 256]]
    ps_ot = ps_bank(7).bitcast(BF16)[:, 0:128]
    smsc_ap = consN[:, 8:9]

    a3 = {"sc": 0, "prow": [0, 0], "ppt": [0, 0], "pst": [0, 0, 0, 0],
          "oa": [0, 0], "ost": 0, "psot": 0, "rs": [0, 0]}

    def emit_vloop(hq, qt, g, nk, t_e, pb):
        P_row = P_rows[pb]
        sum_c = 14 if pb == 0 else 12
        t_rs = em.on("vector", lambda v, sum_c=sum_c: v.reciprocal(stat[:, sum_c + 1:sum_c + 2],
                                                                  stat[:, sum_c:sum_c + 1]),
                     wait=[("AC", t_e)], inc=("DV", 1))
        a3["rs"][pb] = t_rs
        tp_of = {}

        def emit_tp(kt):
            par = kt % 2
            wt_ = [("AC", t_e)]
            if a3["ppt"][par]:
                wt_.append(("DV", a3["ppt"][par]))
            tp_of[kt] = em.on("tensor", lambda t, kt=kt, par=par: t.transpose(
                ps_pt2[par], P_row[:, kt * 128:(kt + 1) * 128], ident[:]),
                  wait=wt_, inc=("PE", 1), tag=f"ptp_{hq}_{qt}_{kt}")

        emit_tp(0)
        if nk > 1:
            emit_tp(1)
        t_mm = 0
        for kt in range(nk):
            par = kt % 2
            wt2 = [("PE", tp_of[kt])]
            if a3["pst"][kt % 4]:
                wt2.append(("PE", a3["pst"][kt % 4]))
            t_cp = em.on("vector", lambda v, par=par, kt=kt: v.tensor_copy(pstage[kt % 4], ps_pt2[par]),
                         wait=wt2, inc=("DV", 1))
            a3["ppt"][par] = t_cp
            wtm = [("DV", t_cp)]
            if kt == 0 and a3["oa"][pb]:
                wtm.append(("DV", a3["oa"][pb]))
            t_mm = em.on("tensor", lambda t, kt=kt, g=g, nk=nk, pb=pb: t.matmul(
                ps_oa2[pb], pstage[kt % 4], v_sb[:, kt * 512 + g * 128: kt * 512 + (g + 1) * 128],
                start=(kt == 0), stop=(kt == nk - 1)), wait=wtm, inc=("PE", 1), tag=f"vm_{hq}_{qt}_{kt}")
            a3["pst"][kt % 4] = t_mm
            if kt + 2 < nk:
                emit_tp(kt + 2)
        a3["prow"][pb] = tp_of[nk - 1] if nk >= 1 else 0
        wt3 = [("PE", t_mm), ("DV", t_rs)]
        if a3["ost"]:
            wt3.append(("PE", a3["ost"]))
        rs_c = 15 if pb == 0 else 13
        t_ev = em.on("vector", lambda v, rs_c=rs_c, pb=pb: v.tensor_scalar(
            ostage, ps_oa2[pb], stat[:, rs_c:rs_c + 1], None, A.mult),
                     wait=wt3, inc=("DV", 1), tag=f"oev_{hq}_{qt}")
        a3["oa"][pb] = t_ev
        wt4 = [("DV", t_ev)]
        if a3["psot"]:
            wt4.append(("DV", a3["psot"]))
        t_ot = em.on("tensor", lambda t: t.transpose(ps_ot, ostage, ident[:]),
                     wait=wt4, inc=("PE", 1))
        a3["ost"] = t_ot
        a3["psot"] = em.on("vector", lambda v, hq=hq, qt=qt: v.tensor_copy(
            attT[:, hq * 2048 + qt * 128: hq * 2048 + (qt + 1) * 128], ps_ot),
              wait=[("PE", t_ot)], inc=("DV", 1))

    pending = None
    itc = 0
    for hq in range(NH):
        g = hq // 4
        for qt in range(NT):
            nk = qt + 1
            ncols = nk * 128
            pb = itc % 2
            lhs = qT[:, hq * T + qt * 128: hq * T + (qt + 1) * 128]
            c = 0
            first = True
            while c < ncols:
                w = min(512, ncols - c)
                wt = []
                if first:
                    if a3["sc"]:
                        wt.append(("AC", a3["sc"]))
                    if itc == 0:
                        wt.append(("DV", dv_a2b))
                    first = False
                inc = ("PE", 1) if c + w >= ncols else None
                em.on("tensor", lambda t, c=c, w=w, lhs=lhs, g=g: t.matmul(
                    ps_sc[:, c:c + w], lhs, kT[:, g * T + c: g * T + c + w], start=True, stop=True),
                      wait=wt, inc=inc, tag=f"sc_{hq}_{qt}_{c}")
                c += w
            t_sc = em.cnt["PE"]
            t_max = em.on("vector", lambda v, ncols=ncols: v.reduce_max(stat[:, 8:9], ps_sc[:, 0:ncols], AX.X),
                          wait=[("PE", t_sc)], inc=("DV", 1), tag=f"mxc_{hq}_{qt}")
            t_msk = em.on("vector", lambda v, nk=nk: v.tensor_tensor(
                ps_sc[:, (nk - 1) * 128: nk * 128], ps_sc[:, (nk - 1) * 128: nk * 128], maskd[:], A.add),
                  wait=[("DV", t_max)], inc=("DV", 1), tag=f"msk_{hq}_{qt}")
            t_bias = em.on("vector", lambda v: v.tensor_scalar(stat[:, 9:10], stat[:, 8:9], -SM_SCALE,
                                                               None, A.mult), wait=[("DV", t_max)], inc=("DV", 1))
            wt_exp = [("DV", t_bias), ("DV", t_msk)]
            if a3["prow"][pb]:
                wt_exp.append(("PE", a3["prow"][pb]))
            if a3["rs"][pb]:
                wt_exp.append(("DV", a3["rs"][pb]))
            sum_c = 14 if pb == 0 else 12
            t_e = em.on("scalar", lambda s, ncols=ncols, pb=pb, sum_c=sum_c: s.activation(
                out=P_rows[pb][:, 0:ncols], in_=ps_sc[:, 0:ncols], func=AF.Exp, bias=stat[:, 9:10],
                scale=smsc_ap, accum_out=stat[:, sum_c:sum_c + 1]), wait=wt_exp, inc=("AC", 1),
                  tag=f"exp_{hq}_{qt}")
            a3["sc"] = t_e
            if pending is not None:
                emit_vloop(*pending)
            pending = (hq, qt, g, nk, t_e, pb)
            itc += 1
    emit_vloop(*pending)
    dv_a3 = em.cnt["DV"]
    pe_a3 = em.cnt["PE"]

    if upto == "A3":
        if "attT" in debug:
            em.dma("sync", dbg_out("attT", [128, NH * T], BF16), attT, sem="ST", wait=[("DV", dv_a3)])
        em.wait("sync", [("ST", em.cnt["ST"]), ("DV", em.cnt["DV"]), ("AC", em.cnt["AC"]),
                         ("PE", em.cnt["PE"]), ("LD2", em.cnt["LD2"])])
        _replay(nc, em)
        for g in reversed(ctx):
            g.__exit__(None, None, None)
        return nc, dbg

    woA = k_st
    woB = v_sb
    xt2 = [scrv(16384, 1024, F32), scrv(20480, 1024, F32)]
    x2st = [scrv(24576, 1024, F32), scrv(28672, 1024, F32)]
    sqdump = scrv(32768, 1024, F32)
    ps_o = PS[:, 0:1024]
    ln2p = qk_sums

    wo_guard = [0, 0]
    xt_guard = [0, 0]
    x2_guard = [0, 0]
    pso_guard = 0
    for hh in range(2):
        wA = [("PE", wo_guard[0])] if wo_guard[0] else [("DV", dv_a3)]
        t_woA = em.dma("gpsimd", woA, bass.AP(woT_d.tensor, hh * 1024,
                       [[H, 128], [128 * H, 8], [1, 1024]]), sem="LDW0", wait=wA)
        t_woB = em.dma("gpsimd", woB, bass.AP(woT_d.tensor, 8 * 128 * H + hh * 1024,
                       [[H, 128], [128 * H, 8], [1, 1024]]), sem="LDW1", wait=wA)
        for tt in range(NT):
            par = tt % 2
            lsem = "LDX0" if par == 0 else "LDX1"
            t_x = em.dma("sync", xt2[par], x_d[tt * 128:(tt + 1) * 128, hh * 1024:(hh + 1) * 1024],
                         sem=lsem, wait=[("DV", xt_guard[par])] if xt_guard[par]
                         else [("DV", dv_a3), ("PE", pe_a3)])
            for oc in range(16):
                wot = (woA if oc < 8 else woB)[:, (oc % 8) * 1024:((oc % 8) + 1) * 1024]
                for h2 in range(2):
                    wt = []
                    if oc == 0 and h2 == 0:
                        wt = [("LDW0", t_woA), ("LDW1", t_woB)]
                        if pso_guard:
                            wt.append(("DV", pso_guard))
                        if hh == 0 and tt == 0:
                            wt.append(("DV", dv_a3))
                    inc = ("PE", 1) if (oc == 15 and h2 == 1) else None
                    em.on("tensor", lambda t, oc=oc, h2=h2, wot=wot, tt=tt: t.matmul(
                        ps_o[:, h2 * 512:(h2 + 1) * 512],
                        attT[:, oc * 2048 + tt * 128: oc * 2048 + (tt + 1) * 128],
                        wot[:, h2 * 512:(h2 + 1) * 512], start=(oc == 0), stop=(oc == 15)),
                          wait=wt, inc=inc, tag=f"wo_{hh}_{tt}_{oc}_{h2}")
            t_pe = em.cnt["PE"]
            wt5 = [("PE", t_pe), (lsem, t_x)]
            if x2_guard[par]:
                wt5.append(("STX0" if par == 0 else "STX1", x2_guard[par]))
            t_add = em.on("vector", lambda v, par=par: v.tensor_tensor(x2st[par], ps_o, xt2[par], A.add),
                          wait=wt5, inc=("DV", 1), tag=f"x2_{hh}_{tt}")
            pso_guard = t_add
            xt_guard[par] = t_add
            em.drain("vector")
            em.on("vector", lambda v, par=par, tt=tt, hh=hh: v.tensor_reduce(
                ln2p[:, tt * 4 + hh: tt * 4 + hh + 1], x2st[par], AX.X, A.add))
            em.on("vector", lambda v, par=par: v.tensor_tensor(sqdump, x2st[par], x2st[par], A.mult))
            em.drain("vector")
            t_st2 = em.on("vector", lambda v, tt=tt, hh=hh: v.tensor_reduce(
                ln2p[:, tt * 4 + 2 + hh: tt * 4 + 3 + hh], sqdump, AX.X, A.add), inc=("DV", 1))
            x2_guard[par] = em.dma("sync", x2_h[tt * 128:(tt + 1) * 128, hh * 1024:(hh + 1) * 1024],
                                   x2st[par], sem="STX0" if par == 0 else "STX1", wait=[("DV", t_st2)])
        wo_guard = [em.cnt["PE"], em.cnt["PE"]]
    st_x2 = (em.cnt["STX0"], em.cnt["STX1"])
    dv_a4 = em.cnt["DV"]

    xfT = R1[:]
    x2full = scrv(16384, 2048, F32)
    xf_s = scrv(24576, 2048, BF16)
    x5_guard = 0
    for tt in range(NT):
        wt = [("STX0", st_x2[0]), ("STX1", st_x2[1])] if tt == 0 else []
        if x5_guard:
            wt.append(("DV", x5_guard))
        t_x2 = em.dma("sync", x2full, x2_h[tt * 128:(tt + 1) * 128, :], sem="LDX0", wait=wt)
        c0 = tt * 4
        em.on("vector", lambda v, c0=c0: v.tensor_tensor(stat[:, 0:1], ln2p[:, c0:c0 + 1],
                                                         ln2p[:, c0 + 1:c0 + 2], A.add),
              wait=[("DV", dv_a4)] if tt == 0 else ())
        em.on("vector", lambda v, c0=c0: v.tensor_tensor(stat[:, 1:2], ln2p[:, c0 + 2:c0 + 3],
                                                         ln2p[:, c0 + 3:c0 + 4], A.add))
        em.drain("vector")
        em.on("vector", lambda v: v.tensor_scalar(stat[:, 2:3], stat[:, 0:1], 1.0 / H, None, A.mult))
        em.drain("vector")
        em.on("vector", lambda v: v.tensor_tensor(stat[:, 3:4], stat[:, 2:3], stat[:, 2:3], A.mult))
        em.drain("vector")
        t_var = em.on("vector", lambda v: v.scalar_tensor_tensor(stat[:, 4:5], stat[:, 1:2], 1.0 / H,
                                                                 stat[:, 3:4], A.mult, A.subtract),
                      inc=("DV", 1))
        t_sd = em.on("scalar", lambda s: s.activation(out=stat[:, 5:6], in_=stat[:, 4:5], func=AF.Sqrt,
                                                      bias=epsc[:]), wait=[("DV", t_var)], inc=("AC", 1))
        em.on("vector", lambda v: v.reciprocal(stat[:, 6:7], stat[:, 5:6]), wait=[("AC", t_sd)])
        em.drain("vector")
        t_xf = em.on("vector", lambda v: v.tensor_scalar(xf_s, x2full, stat[:, 2:3], stat[:, 6:7],
                                                         A.subtract, A.mult),
                     wait=[("LDX0", t_x2)], inc=("DV", 1))
        x5_guard = t_xf
        for c8 in range(8):
            for parn in range(2):
                kk = 2 * c8 + parn
                pt = ps_tp[kk % 2]
                src = xf_s[:, 256 * c8 + parn: 256 * c8 + 256: 2]
                dst = xfT[:, kk * T + tt * 128: kk * T + (tt + 1) * 128]
                wt6 = [("DV", t_xf)]
                if tp_guard[kk % 2]:
                    wt6.append(("DV", tp_guard[kk % 2]))
                t_pe = em.on("tensor", lambda t, pt=pt, src=src: t.transpose(pt, src, ident[:]),
                             wait=wt6, inc=("PE", 1), tag=f"tpXF_{tt}_{kk}")
                tp_guard[kk % 2] = em.on("vector", lambda v, pt=pt, dst=dst: v.tensor_copy(dst, pt),
                                         wait=[("PE", t_pe)], inc=("DV", 1))
        em.wait("vector", [("PE", em.cnt["PE"])])
    dv_a5 = em.cnt["DV"]

    if upto == "A5":
        if "xfT" in debug:
            em.dma("sync", dbg_out("xfT", [128, NHC * T], BF16), xfT, sem="ST", wait=[("DV", dv_a5)])
        if "x2" in debug:
            em.dma("sync", dbg_out("x2", [T, H], F32), x2_h, sem="ST",
                   wait=[("STX0", st_x2[0]), ("STX1", st_x2[1])])
        em.wait("sync", [("ST", em.cnt["ST"]), ("DV", em.cnt["DV"]), ("AC", em.cnt["AC"]),
                         ("PE", em.cnt["PE"]), ("LD2", em.cnt["LD2"])])
        _replay(nc, em)
        for g in reversed(ctx):
            g.__exit__(None, None, None)
        return nc, dbg


    pe_a5 = em.cnt["PE"]
    ext_s = [[scrv(o, 256, U32), scrv(o + 1024, 256, U32), scrv(o + 2048, 256, U32)]
             for o in (0, 3072)]
    ext_u8 = [[scrv(o, 1024, U8), scrv(o + 1024, 1024, U8), scrv(o + 2048, 1024, U8)]
              for o in (0, 3072)]
    tmp = [[scrv(o, 1024, BF16), scrv(o + 2048, 1024, BF16), scrv(o + 4096, 1024, BF16),
            scrv(o + 6144, 1024, BF16)] for o in (6144, 14336)]
    interA = scrv(26624, 2048, BF16)
    interB = scrv(30720, 2048, BF16)
    R3u8 = R3[:].bitcast(U8)
    R2b = R2[:]
    ps_g = PS[:, 0:2048]
    ps_u = PS[:, 2048:4096]

    cHALF = consN[:, 0:1]; cM2 = consN[:, 1:2]; cM6 = consN[:, 2:3]
    c32 = consN[:, 3:4]; c16 = consN[:, 4:5]; cM1 = consN[:, 5:6]
    c4 = consN[:, 6:7]; c127 = consN[:, 7:8]; cM4 = consN[:, 9:10]
    LO_SC = ((None, cM4), (None, cM6))
    HI_SC = ((c16, cM4), (c16, cM6))

    extr_guard = [0, 0]
    scl_guard = 0
    valset_guard = [pe_a5, pe_a5]
    tmpset_guard = [0, 0]
    st_dq = {"scl": 0, "valwait": None}

    def emit_acts(u, nib, B8set, scales):
        tm = tmp[u % 2]
        eu = ext_u8[u % 2]
        msrc = eu[0] if nib == 0 else eu[1]
        (r4sb, r6sb) = scales
        wta = list(dq_act_wait.pop((u, nib), ()))
        em.on("scalar", lambda s, msrc=msrc, tm=tm, sc=r4sb: (
            s.activation(out=tm[1], in_=msrc, func=AF.Relu, scale=sc[0], bias=sc[1]) if sc[0] is not None
            else s.activation(out=tm[1], in_=msrc, func=AF.Relu, bias=sc[1])),
              wait=wta, tag=f"r4_{u}_{nib}")
        em.on("scalar", lambda s, msrc=msrc, tm=tm, sc=r6sb: (
            s.activation(out=tm[2], in_=msrc, func=AF.Relu, scale=sc[0], bias=sc[1]) if sc[0] is not None
            else s.activation(out=tm[2], in_=msrc, func=AF.Relu, bias=sc[1])))
        if nib == 0:
            return em.on("scalar", lambda s, tm=tm, eu=eu: s.activation(
                out=tm[3], in_=eu[2], func=AF.Sign, scale=cM1, bias=c4), inc=("AC", 1), tag=f"sg_{u}_{nib}")
        return em.on("scalar", lambda s, tm=tm, B8set=B8set: s.activation(
            out=tm[3], in_=B8set, func=AF.Sign, scale=cM1, bias=c127), inc=("AC", 1), tag=f"sg_{u}_{nib}")

    dq_act_wait = {}

    def dequant_pair(u, v, B8u, B32u, B8v, B32v, scu, scv, wouts_u, wouts_v, first_wait):
        for (uu, B32) in ((u, B32u), (v, B32v)):
            es = ext_s[uu % 2]
            wt = list(first_wait) if uu == u else []
            if ext_guard2[uu % 2]:
                wt.append(("AC", ext_guard2[uu % 2]))
            em.on("vector", lambda vv, B32=B32, d=es[0]: vv.tensor_scalar(d, B32, 0x07070707, None, A.bitwise_and),
                  wait=wt, tag=f"ext0_{uu}")
            em.on("vector", lambda vv, B32=B32, d=es[1]: vv.tensor_scalar(d, B32, 0x70707070, None, A.bitwise_and))
            t_e = em.on("vector", lambda vv, B32=B32, d=es[2]: vv.tensor_scalar(d, B32, 0x08080808, None, A.bitwise_and),
                        inc=("DV", 1), tag=f"ext2_{uu}")
            ext_done[uu % 2] = t_e
        last_w = 0
        for nib in range(2):
            acts = {}
            for uu, B8, sc in ((u, B8u, scu), (v, B8v, scv)):
                wta = [("DV", ext_done[uu % 2])]
                if nib == 1:
                    wta.append(("DV", w_of[(uu, 0)]))
                elif tmpset_guard[uu % 2]:
                    wta.append(("DV", tmpset_guard[uu % 2]))
                dq_act_wait[(uu, nib)] = wta
                acts[uu] = emit_acts(uu, nib, B8, LO_SC if nib == 0 else HI_SC)
            tvals = {}
            for uu, sc_sl in ((u, scu_sl[0]), (v, scu_sl[1])):
                tm = tmp[uu % 2]
                wtb = [("AC", acts[uu])]
                if uu == u and nib == 0:
                    if st_dq["scl"]:
                        wtb.append(("LDS", st_dq["scl"]))
                    if st_dq["valwait"] is not None:
                        wtb.append(("PE", st_dq["valwait"]))
                        st_dq["valwait"] = None
                tvals[(uu, "A")] = em.on("vector", lambda vv, tm=tm: vv.scalar_tensor_tensor(
                    tm[1], tm[2], 2.0, tm[1], A.mult, A.add), wait=wtb, inc=("DV", 1), tag=f"A_{uu}_{nib}")
                tvals[(uu, "fs")] = em.on("vector", lambda vv, tm=tm, s=sc_sl: vv.scalar_tensor_tensor(
                    tm[3], tm[3], 0.5, s, A.mult, A.mult), inc=("DV", 1), tag=f"fs_{uu}_{nib}")
            for uu in (u, v):
                tm = tmp[uu % 2]
                eu = ext_u8[uu % 2]
                msrc = eu[0] if nib == 0 else eu[1]
                kmul = 1.0 if nib == 0 else 1.0 / 16
                tvals[(uu, "mag")] = em.on("vector", lambda vv, tm=tm, msrc=msrc, kmul=kmul: vv.scalar_tensor_tensor(
                    tm[1], msrc, kmul, tm[1], A.mult, A.add),
                    wait=[("DV", tvals[(uu, "A")])], inc=("DV", 1), tag=f"mag_{uu}_{nib}")
            for uu, wouts in ((u, wouts_u), (v, wouts_v)):
                tm = tmp[uu % 2]
                wout = wouts[nib]
                last_w = em.on("vector", lambda vv, tm=tm, wout=wout: vv.tensor_tensor(wout, tm[1], tm[3], A.mult),
                               wait=[("DV", tvals[(uu, "mag")]), ("DV", tvals[(uu, "fs")])],
                               inc=("DV", 1), tag=f"w_{uu}_{nib}")
                w_of[(uu, nib)] = last_w
                tmpset_guard[uu % 2] = last_w
        return last_w

    ext_done = [0, 0]
    ext_guard2 = [0, 0]
    w_of = {}
    scu_sl = [None, None]

    interA_guard = 0
    interB_guard = 0
    psg_guard = 0
    psu_guard = 0
    uglobal = 0

    def emit_deq_B_dmas(ch):
        nonlocal scl_guard
        p = ch % 2
        bsem = "LDB0" if p == 0 else "LDB1"
        bw = [("DV", extr_guard[p])] if extr_guard[p] else [("PE", pe_a5)]
        em.dma("sync", R3u8[:, p * 8192: p * 8192 + 4096],
               bass.AP(gby_d.tensor, ch * 512, [[MLP, 128], [128 * MLP, 8], [1, 512]]),
               sem=bsem, wait=bw)
        t_b = em.dma("sync", R3u8[:, p * 8192 + 4096: p * 8192 + 8192],
                     bass.AP(uby_d.tensor, ch * 512, [[MLP, 128], [128 * MLP, 8], [1, 512]]),
                     sem=bsem, wait=())
        sw = [("DV", scl_guard)] if scl_guard else [("PE", pe_a5)]
        t_s = None
        for mi2, sd in enumerate((gsc_d, usc_d)):
            t_s = em.dma("gpsimd", R4[:, mi2 * 4096: (mi2 + 1) * 4096],
                         bass.AP(sd.tensor, ch * 512, [[MLP, 128], [128 * MLP, 8], [1, 512]]),
                         sem="LDS", wait=sw if mi2 == 0 else ())
        st_dq["scl"] = t_s
        st_dq["valwait"] = valset_guard[p]
        return (p, t_b, bsem)

    def emit_deq_B_pair(ch, k, dmainfo, t_deq_mi):
        nonlocal uglobal, scl_guard
        p, t_b, bsem = dmainfo
        mi, gp = k // 2, k % 2
        base_b = p * 8192 + mi * 4096
        vset_off = p * 16384 + mi * 8192
        prs = []
        for gr in (2 * gp, 2 * gp + 1):
            B8 = R3u8[:, base_b + gr * 1024: base_b + (gr + 1) * 1024]
            B32 = R3u8.bitcast(U32)[:, (base_b + gr * 1024) // 4: (base_b + (gr + 1) * 1024) // 4]
            sc_sl = R4[:, mi * 4096 + gr * 1024: mi * 4096 + (gr + 1) * 1024]
            wl = bass.AP(R2, vset_off + (4 * gr) * 512, [[32768, 128], [1024, 2], [1, 512]])
            wh = bass.AP(R2, vset_off + (4 * gr + 1) * 512, [[32768, 128], [1024, 2], [1, 512]])
            prs.append((B8, B32, sc_sl, (wl, wh)))
        scu_sl[0], scu_sl[1] = prs[0][2], prs[1][2]
        fw = [(bsem, t_b)] if k == 0 else []
        t_w = dequant_pair(uglobal, uglobal + 1, prs[0][0], prs[0][1], prs[1][0], prs[1][1],
                           prs[0][2], prs[1][2], prs[0][3], prs[1][3], fw)
        uglobal += 2
        if k == 1:
            t_deq_mi[0] = t_w
        if k == 3:
            t_deq_mi[1] = t_w
            extr_guard[p] = t_w
            scl_guard = t_w

    bstate = {"interA": 0, "interB": 0, "psg": 0, "psu": 0}

    def emit_compute_B_nt(ch, p, t_deq_mi, nt):
        if True:
            blk, parn = nt // 2, nt % 2
            r = ch * 4 + nt
            t_mm_gate = 0
            for mi in range(2):
                pst = ps_g if mi == 0 else ps_u
                vset_off = p * 16384 + mi * 8192
                for tc in range(4):
                    for kk in range(16):
                        lhs = R2b[:, vset_off + kk * 512 + blk * 256 + parn:
                                  vset_off + kk * 512 + (blk + 1) * 256: 2]
                        rhs = xfT[:, kk * T + tc * 512: kk * T + (tc + 1) * 512]
                        wt = []
                        if tc == 0 and kk == 0:
                            if nt == 0:
                                wt.append(("DV", t_deq_mi[mi]))
                                if ch == 0 and mi == 0:
                                    wt.append(("DV", dv_a5))
                            if mi == 0 and bstate["psg"]:
                                wt.append(("AC", bstate["psg"]))
                            if mi == 1 and bstate["psu"]:
                                wt.append(("DV", bstate["psu"]))
                        inc = ("PE", 1) if (tc == 3 and kk == 15) else None
                        em.on("tensor", lambda t, pst=pst, tc=tc, lhs=lhs, rhs=rhs, kk=kk: t.matmul(
                            pst[:, tc * 512:(tc + 1) * 512], lhs, rhs, start=(kk == 0), stop=(kk == 15)),
                              wait=wt, inc=inc, tag=f"ffn_{ch}_{nt}_{mi}_{tc}_{kk}")
                if mi == 0:
                    t_mm_gate = em.cnt["PE"]
            t_mm = em.cnt["PE"]
            wtg = [("PE", t_mm_gate)]
            if bstate["interA"]:
                wtg.append(("DV", bstate["interA"]))
            t_gelu = em.on("scalar", lambda s: s.activation(out=interA, in_=ps_g, func=AF.Gelu_apprx_tanh),
                           wait=wtg, inc=("AC", 1), tag=f"gelu_{ch}_{nt}")
            bstate["psg"] = t_gelu
            wtm2 = [("AC", t_gelu)]
            if bstate["interB"]:
                wtm2.append(("STI", bstate["interB"]))
            wtm2.append(("PE", t_mm))
            t_mult = em.on("vector", lambda v: v.tensor_tensor(interB, interA, ps_u, A.mult),
                           wait=wtm2, inc=("DV", 1), tag=f"imul_{ch}_{nt}")
            bstate["interA"] = t_mult
            bstate["psu"] = t_mult
            bstate["interB"] = em.dma("sync", intT_h[r * 128:(r + 1) * 128, :], interB, sem="STI",
                                      wait=[("DV", t_mult)])
        if nt == 3:
            valset_guard[p] = em.cnt["PE"]

    bst = []
    for ch in range(33):
        dmainfo = None
        if ch < 32:
            dmainfo = emit_deq_B_dmas(ch)
            bst.append((dmainfo, [0, 0]))
        for k in range(4):
            if ch < 32:
                emit_deq_B_pair(ch, k, bst[ch][0], bst[ch][1])
            if ch >= 1:
                emit_compute_B_nt(ch - 1, bst[ch - 1][0][0], bst[ch - 1][1], k)
    pe_b = em.cnt["PE"]
    sti_total = em.cnt["STI"]

    if upto == "B":
        em.wait("sync", [("STI", sti_total), ("DV", em.cnt["DV"]), ("AC", em.cnt["AC"]),
                         ("PE", em.cnt["PE"]), ("ST", em.cnt["ST"])])
        if "inter" in debug:
            em.dma("sync", dbg_out("inter", [MLP, T], BF16), intT_h, sem="ST",
                   wait=[("STI", sti_total)])
            em.wait("sync", [("ST", em.cnt["ST"])])
        _replay(nc, em)
        for g in reversed(ctx):
            g.__exit__(None, None, None)
        return nc, dbg

    accF = R2[:].bitcast(F32)
    WdV = [R1[:, 0:16384], R1[:, 16384:32768]]
    ist = [scrv(22528, 4096, BF16), scrv(30720, 4096, BF16)]
    ps_d = [PS[:, 0:1024], PS[:, 1024:2048]]

    valset_guard_c = [pe_b, pe_b]
    ist_guard = [0, 0]
    psd_guard = [0, 0]
    cstate = {"sto_bar": None}
    first_ci = [True, True]

    def emit_deq_C(hh, sup):
        nonlocal uglobal, scl_guard
        p = sup % 2
        bsem = "LDB0" if p == 0 else "LDB1"
        t_b = em.dma("sync", R3u8[:, p * 8192:(p + 1) * 8192],
                     bass.AP(dby_d.tensor, sup * 1024 * H + hh * 1024,
                             [[H, 128], [128 * H, 8], [1, 1024]]),
                     sem=bsem, wait=[("DV", extr_guard[p])])
        t_s = em.dma("gpsimd", R4[:],
                     bass.AP(dsc_d.tensor, sup * 1024 * H + hh * 1024,
                             [[H, 128], [128 * H, 8], [1, 1024]]),
                     sem="LDS", wait=[("DV", scl_guard)])
        st_dq["scl"] = t_s
        st_dq["valwait"] = valset_guard_c[p]
        return (p, t_b, bsem)

    def emit_deq_C_pair(hh, sup, gp, dmainfo, tws):
        nonlocal uglobal, scl_guard
        p, t_b, bsem = dmainfo
        prs = []
        for kt in (2 * gp, 2 * gp + 1):
            B8 = R3u8[:, p * 8192 + kt * 1024: p * 8192 + (kt + 1) * 1024]
            B32 = R3u8.bitcast(U32)[:, (p * 8192 + kt * 1024) // 4: (p * 8192 + (kt + 1) * 1024) // 4]
            sc_sl = R4[:, kt * 1024:(kt + 1) * 1024]
            vset = WdV[p]
            wl = vset[:, (2 * kt) * 1024:(2 * kt + 1) * 1024]
            wh = vset[:, (2 * kt + 1) * 1024:(2 * kt + 2) * 1024]
            prs.append((B8, B32, sc_sl, (wl, wh)))
        scu_sl[0], scu_sl[1] = prs[0][2], prs[1][2]
        fw = [(bsem, t_b)] if gp == 0 else []
        t_w = dequant_pair(uglobal, uglobal + 1, prs[0][0], prs[0][1], prs[1][0], prs[1][1],
                           prs[0][2], prs[1][2], prs[0][3], prs[1][3], fw)
        uglobal += 2
        if gp == 3:
            extr_guard[p] = t_w
            scl_guard = t_w
        tws[gp] = t_w

    def emit_compute_C_tp(hh, sup, p, t_w, tp2):
        if True:
            tpp = tp2 % 2
            isem = "LDI0" if tpp == 0 else "LDI1"
            iw = [("DV", ist_guard[tpp])] if ist_guard[tpp] else []
            if first_ci[tpp]:
                iw.append(("STI", sti_total))
                first_ci[tpp] = False
            jsem = "LDJ0" if tpp == 0 else "LDJ1"
            t_i = em.dma("sync", ist[tpp][:, 0:2048],
                         bass.AP(intT_h.tensor, (sup * 16) * 128 * T + tp2 * 256,
                                 [[T, 128], [128 * T, 8], [1, 256]]),
                         sem=isem, wait=iw)
            t_j = em.dma("gpsimd", ist[tpp][:, 2048:4096],
                         bass.AP(intT_h.tensor, (sup * 16 + 8) * 128 * T + tp2 * 256,
                                 [[T, 128], [128 * T, 8], [1, 256]]),
                         sem=jsem, wait=iw)
            for sub in range(2):
                tt = tp2 * 2 + sub
                tpd = tt % 2
                for i in range(16):
                    for h2 in range(2):
                        wt = []
                        if i == 0 and h2 == 0:
                            wt = [(isem, t_i), (jsem, t_j), ("DV", t_w)]
                            if psd_guard[tpd]:
                                wt.append(("DV", psd_guard[tpd]))
                        inc = ("PE", 1) if (i == 15 and h2 == 1) else None
                        em.on("tensor", lambda t, tpd=tpd, i=i, h2=h2, p=p, tpp=tpp, sub=sub: t.matmul(
                            ps_d[tpd][:, h2 * 512:(h2 + 1) * 512],
                            ist[tpp][:, i * 256 + sub * 128: i * 256 + sub * 128 + 128],
                            WdV[p][:, i * 1024 + h2 * 512: i * 1024 + h2 * 512 + 512],
                            start=(i == 0), stop=(i == 15)), wait=wt, inc=inc,
                              tag=f"dn_{hh}_{sup}_{tt}_{i}_{h2}")
                t_mm = em.cnt["PE"]
                acc_t = accF[:, tt * 1024:(tt + 1) * 1024]
                if sup == 0:
                    wt8 = [("PE", t_mm)]
                    if hh == 0 and tt == 0:
                        wt8.append(("PE", pe_b))
                    if cstate["sto_bar"] is not None and tt == 0:
                        wt8.extend(cstate["sto_bar"])
                    t_ev = em.on("vector", lambda v, acc_t=acc_t, tpd=tpd: v.tensor_copy(acc_t, ps_d[tpd]),
                                 wait=wt8, inc=("DV", 1), tag=f"acc0_{hh}_{tt}")
                elif sup < 7:
                    t_ev = em.on("vector", lambda v, acc_t=acc_t, tpd=tpd: v.tensor_tensor(
                        acc_t, acc_t, ps_d[tpd], A.add), wait=[("PE", t_mm)], inc=("DV", 1),
                          tag=f"acc_{hh}_{sup}_{tt}")
                else:
                    t_ev = em.on("vector", lambda v, acc_t=acc_t, tpd=tpd: v.tensor_tensor(
                        acc_t, acc_t, ps_d[tpd], A.add), wait=[("PE", t_mm)], inc=("DV", 1),
                          tag=f"accL_{hh}_{tt}")
                    em.dma("sync", out_d[tt * 128:(tt + 1) * 128, hh * 1024:(hh + 1) * 1024],
                           acc_t, sem="STO0" if tpd == 0 else "STO1", wait=[("DV", t_ev)])
                psd_guard[tpd] = t_ev
                ist_guard[tpp] = t_ev
        if tp2 == 7:
            valset_guard_c[p] = em.cnt["PE"]
            if sup == 7:
                cstate["sto_bar"] = [("STO0", em.cnt["STO0"]), ("STO1", em.cnt["STO1"])]

    cst = []
    for s in range(17):
        if s < 16:
            cst.append([(s // 8, s % 8), emit_deq_C(s // 8, s % 8), [0, 0, 0, 0]])
        for k in range(8):
            if s < 16 and k < 4:
                emit_deq_C_pair(cst[s][0][0], cst[s][0][1], k, cst[s][1], cst[s][2])
            if s >= 1:
                (phh, psup), pdma, ptws = cst[s - 1]
                emit_compute_C_tp(phh, psup, pdma[0], ptws[3], k)
    x2f = [scrv(6144, 2048, F32), scrv(22528, 2048, F32)]
    outf = [scrv(14336, 2048, F32), scrv(30720, 2048, F32)]
    sto_all = [("STO0", em.cnt["STO0"]), ("STO1", em.cnt["STO1"])]
    fin_guard = [0, 0]
    fin_st = [0, 0]
    for tt in range(NT):
        fp_ = tt % 2
        lsem = "LDX0" if fp_ == 0 else "LDX1"
        osem = "LDI0" if fp_ == 0 else "LDI1"
        fsem = "STF0" if fp_ == 0 else "STF1"
        g = [("DV", fin_guard[fp_]), (fsem, fin_st[fp_])] if fin_guard[fp_] else (sto_all + [("DV", em.cnt["DV"])])
        t_xl = em.dma("sync", x2f[fp_], x2_h[tt * 128:(tt + 1) * 128, :], sem=lsem, wait=g)
        t_ol = em.dma("sync", outf[fp_], out_d[tt * 128:(tt + 1) * 128, :], sem=osem, wait=g)
        t_ad = em.on("vector", lambda v, fp_=fp_: v.tensor_tensor(outf[fp_], outf[fp_], x2f[fp_], A.add),
                     wait=[(lsem, t_xl), (osem, t_ol)], inc=("DV", 1), tag=f"fin_{tt}")
        fin_guard[fp_] = t_ad
        fin_st[fp_] = em.dma("sync", out_d[tt * 128:(tt + 1) * 128, :], outf[fp_], sem=fsem,
                             wait=[("DV", t_ad)])

    em.wait("sync", [("ST", em.cnt["ST"]), ("STF0", em.cnt["STF0"]), ("STF1", em.cnt["STF1"]),
                     ("STO0", em.cnt["STO0"]), ("STO1", em.cnt["STO1"]),
                     ("DV", em.cnt["DV"]), ("AC", em.cnt["AC"]),
                     ("PE", em.cnt["PE"]), ("LD2", em.cnt["LD2"]), ("LDS", em.cnt["LDS"]),
                     ("STI", em.cnt["STI"]), ("LDI0", em.cnt["LDI0"]), ("LDI1", em.cnt["LDI1"])])

    _replay(nc, em)
    for g in reversed(ctx):
        g.__exit__(None, None, None)
    return nc, dbg


def _replay(nc, em):
    sems = {}
    semctx = []
    for name in sorted(em.cnt.keys()):
        g = nc.semaphore(name)
        sems[name] = g.__enter__()
        semctx.append(g)
    with nc.Block() as block:
        def runner(eng_obj, key):
            for fn, waits, inc, tag in em.ops[key]:
                seen = {}
                for s, val in waits:
                    if val > 0 and seen.get(s, -1) < val:
                        eng_obj.wait_ge(sems[s], val)
                        seen[s] = val
                if fn is None:
                    continue
                if fn == "DRAIN":
                    eng_obj.drain()
                    continue
                r = fn(eng_obj)
                if inc is not None:
                    r.then_inc(sems[inc[0]], inc[1])
        @block.sync
        def _(e): runner(e, "sync")
        @block.vector
        def _(e): runner(e, "vector")
        @block.scalar
        def _(e): runner(e, "scalar")
        @block.tensor
        def _(e): runner(e, "tensor")
        @block.gpsimd
        def _(e): runner(e, "gpsimd")
    for g in reversed(semctx):
        g.__exit__(None, None, None)



def _host_prep(inputs):
    x = np.ascontiguousarray(np.asarray(inputs["L0_attn_hidden_states"], dtype=np.float32))
    d = {}
    d["wqT"] = np.ascontiguousarray(np.asarray(inputs["L0_attn_wq"], dtype=np.float32).T)
    d["wkT"] = np.ascontiguousarray(np.asarray(inputs["L0_attn_wk"], dtype=np.float32).T)
    d["wvT"] = np.ascontiguousarray(np.asarray(inputs["L0_attn_wv"], dtype=np.float32).T)
    d["woT"] = np.ascontiguousarray(np.asarray(inputs["L0_attn_wo"], dtype=np.float32).T)
    d["qw"] = np.ascontiguousarray(np.asarray(inputs["L0_attn_q_norm_w"], dtype=np.float32).reshape(1, -1))
    d["kw"] = np.ascontiguousarray(np.asarray(inputs["L0_attn_k_norm_w"], dtype=np.float32).reshape(1, -1))

    def bytes_T(p):
        a = np.ascontiguousarray(np.asarray(p, dtype=np.int32))
        b = a.view(np.uint8)[:, 0::4]
        return np.ascontiguousarray(b.T)

    d["gby"] = bytes_T(inputs["L0_ffn_gate_W_packed"])
    d["uby"] = bytes_T(inputs["L0_ffn_up_W_packed"])
    d["dby"] = bytes_T(inputs["L0_ffn_down_W_packed"])
    d["gsc"] = np.ascontiguousarray(np.repeat(np.asarray(inputs["L0_ffn_gate_scales"], dtype=np.float32).T, 16, axis=0))
    d["usc"] = np.ascontiguousarray(np.repeat(np.asarray(inputs["L0_ffn_up_scales"], dtype=np.float32).T, 16, axis=0))
    d["dsc"] = np.ascontiguousarray(np.repeat(np.asarray(inputs["L0_ffn_down_scales"], dtype=np.float32).T, 16, axis=0))
    d["identD"] = np.eye(128).astype(NP_BF16)
    inv_freq = (1.0 / 10000.0 ** (np.arange(0, HD, 2, dtype=np.float32) / HD)).astype(np.float32)
    tpos = np.arange(T, dtype=np.float32)
    freqs = np.outer(tpos, inv_freq).astype(np.float32)
    cos = np.cos(freqs).astype(np.float32); sin = np.sin(freqs).astype(np.float32)
    d["cosTD"] = np.ascontiguousarray(cos.reshape(NT, 128, 64).transpose(1, 0, 2).reshape(128, NT * 64))
    d["sinTD"] = np.ascontiguousarray(sin.reshape(NT, 128, 64).transpose(1, 0, 2).reshape(128, NT * 64))
    mi = np.arange(128)
    d["maskdD"] = np.where(mi[None, :] <= mi[:, None], 0.0, -1e30).astype(np.float32)
    d["epscD"] = np.full((128, 1), EPS, np.float32)
    cN = np.zeros((128, 12), np.float32)
    for i, vv in enumerate([0.5, -2.0, -6.0, 1.0 / 32, 1.0 / 16, -1.0, 4.0, 127.5, SM_SCALE, -4.0]):
        cN[:, i] = vv
    d["consND"] = cN
    return x, d


def run_dev(inputs, debug=(), cores=(0,), upto="ALL", nchunks=None, nosq=False):
    x, shared = _host_prep(inputs)
    nc, dbg = build(debug=debug, upto=upto, nchunks=nchunks, nosq=nosq)
    in_maps = []
    for c in cores:
        m = dict(shared)
        m["x"] = np.ascontiguousarray(x[c])
        in_maps.append(m)
    return run_bass_kernel_spmd(nc, in_maps, core_ids=list(cores))


def kernel(**inputs):
    x, shared = _host_prep(inputs)
    nc, _ = build()
    in_maps = []
    for c in range(8):
        m = dict(shared)
        m["x"] = np.ascontiguousarray(x[c])
        in_maps.append(m)
    res = run_bass_kernel_spmd(nc, in_maps, core_ids=list(range(8)))
    return np.stack([np.asarray(r["out"]) for r in res.results]).astype(np.float32)
